# version 01
# baseline (speedup 1.0000x reference)
import numpy as np
import ml_dtypes
from contextlib import ExitStack

import concourse.bass as bass
import concourse.mybir as mybir
from concourse.bass_utils import run_bass_kernel_spmd

F32 = mybir.dt.float32
BF16 = mybir.dt.bfloat16
F16 = mybir.dt.float16
I32 = mybir.dt.int32
AF = mybir.ActivationFunctionType
ALU = mybir.AluOpType
AX = mybir.AxisListType
BF = ml_dtypes.bfloat16

S = 2048
E = 2048
NCORES = 8
NT = S // 128
EC = E // 128
ISCALE = 0.08838834764831845

ENGS = ("tensor", "vector", "scalar", "gpsimd", "sync")
DMAQS = {"sync": "dsy", "gpsimd": "dgp", "scalar": "dsc"}
NDSEM = 12


class Em:

    def __init__(self, nc, block, ctx):
        self.nc = nc
        self.block = block
        self.sems = {}
        for e in ENGS:
            self.sems[e] = ctx.enter_context(nc.semaphore("sem_" + e))
        for q in DMAQS.values():
            for i in range(NDSEM):
                self.sems[q + str(i)] = ctx.enter_context(
                    nc.semaphore("sem_" + q + str(i)))
        self.cnt = {k: 0 for k in self.sems}
        self.hiw = {e: {} for e in ENGS}
        self.qidx = {q: 0 for q in DMAQS.values()}
        self.last_vc = {}
        self.tok_vc = {}
        self.psum_ops = {}
        self.ocount = 0

    def _waits(self, eng, deps):
        need = {}
        stack = list(deps)
        while stack:
            d = stack.pop()
            if d is None:
                continue
            if isinstance(d, list):
                stack.extend(d)
                continue
            s, v = d
            if need.get(s, 0) < v:
                need[s] = v
        out = []
        hw = self.hiw[eng]
        for s, v in sorted(need.items()):
            if s == eng and v <= self.cnt[eng]:
                pass
            if hw.get(s, 0) >= v:
                continue
            hw[s] = v
            out.append((self.sems[s], v))
        return out

    AUDIT = False

    def _vc_new(self, eng, deps):
        vc = dict(self.last_vc.get(eng) or {})
        stack = list(deps)
        while stack:
            d = stack.pop()
            if d is None:
                continue
            if isinstance(d, list):
                stack.extend(d)
                continue
            sname, v = d
            if vc.get(sname, 0) < v:
                vc[sname] = v
            pvc = self.tok_vc.get((sname, v))
            if pvc:
                for k2, v2 in pvc.items():
                    if vc.get(k2, 0) < v2:
                        vc[k2] = v2
        return vc

    def _audit_rec(self, eng, tok, vc, psum_tensors, is_pe_write):
        self.last_vc[eng] = vc
        eff = tok if tok is not None else (eng, self.cnt[eng] + 1)
        if tok is not None:
            tvc = dict(vc)
            tvc[tok[0]] = tok[1]
            self.tok_vc[tok] = tvc
        for tname in psum_tensors:
            lst = self.psum_ops.setdefault(tname, [])
            if is_pe_write:
                for (e2, oc2, eff2, vc2, w2) in lst[-400:]:
                    if e2 == "tensor":
                        continue
                    if not self._ordered(eff2, vc, vc2, eff):
                        print(f"PSUM-BANK-RACE {tname}: PE#{self.ocount} vs {e2}#{oc2}")
                lst.append((eng, self.ocount, eff, vc, True))
            else:
                for (e2, oc2, eff2, vc2, w2) in lst[-400:]:
                    if e2 != "tensor" or not w2:
                        continue
                    if not self._ordered(eff2, vc, vc2, eff):
                        print(f"PSUM-BANK-RACE {tname}: {eng}#{self.ocount} vs PE#{oc2}")
                lst.append((eng, self.ocount, eff, vc, False))
        self.ocount += 1

    def _ordered(self, tokA, vcB, vcA, tokB):
        if tokA is not None and vcB.get(tokA[0], 0) >= tokA[1]:
            return True
        if tokB is not None and vcA.get(tokB[0], 0) >= tokB[1]:
            return True
        return False

    def op(self, eng, fn, deps=(), inc=True, psum=()):
        waits = self._waits(eng, deps)
        tok = None
        if inc:
            self.cnt[eng] += 1
            tok = (eng, self.cnt[eng])
        sem = self.sems[eng]

        cap = {}

        def body(Eh):
            for s, v in waits:
                Eh.wait_ge(s, v)
            inst = fn(Eh)
            cap["inst"] = inst
            if tok is not None:
                inst.then_inc(sem, 1)

        getattr(self.block, eng)(body)
        if self.AUDIT:
            vc = self._vc_new(eng, deps)
            names = set(psum)
            inst = cap.get("inst")
            mi = getattr(inst, "ins", None)
            if mi is not None:
                for lst_ in (list(getattr(mi, "ins", []) or []),
                             list(getattr(mi, "outs", []) or [])):
                    for ap in lst_:
                        nm = getattr(ap, "memref", None)
                        if isinstance(nm, str) and nm.startswith("ps"):
                            names.add(nm)
            self._audit_rec(eng, tok, vc, names, eng == "tensor")
        return tok

    def wait(self, eng, deps):
        if self.AUDIT:
            self.last_vc[eng] = self._vc_new(eng, deps)
        waits = self._waits(eng, deps)
        if not waits:
            return

        def body(Eh):
            for s, v in waits:
                Eh.wait_ge(s, v)

        getattr(self.block, eng)(body)

    def dma(self, queue, out, in_, deps=()):
        q = DMAQS[queue]
        sq = q + str(self.qidx[q])
        self.qidx[q] = (self.qidx[q] + 1) % NDSEM
        fence = (sq, self.cnt[sq]) if self.cnt[sq] > 0 else None
        waits = self._waits(queue, list(deps) + [fence])
        self.cnt[sq] += 16
        tok = (sq, self.cnt[sq])
        sem = self.sems[sq]

        def body(Eh):
            for s, v in waits:
                Eh.wait_ge(s, v)
            Eh.dma_start(out=out, in_=in_).then_inc(sem, 16)

        getattr(self.block, queue)(body)
        if self.AUDIT:
            vc = self._vc_new(queue, list(deps) + [fence])
            self.last_vc[queue] = vc
            tvc = dict(vc)
            tvc[sq] = tok[1]
            self.tok_vc[tok] = tvc
        return tok

    def barrier(self):
        toks = [(k, v) for k, v in self.cnt.items() if v > 0]
        for e in ENGS:
            self.wait(e, toks)


def host_prep(inp):
    d = {}
    d["x"] = np.ascontiguousarray(inp["L0_attn_x"])
    d["wq"] = np.ascontiguousarray(inp["L0_attn_Wq"])
    d["wk"] = np.ascontiguousarray(inp["L0_attn_Wk"])
    d["wv"] = np.ascontiguousarray(inp["L0_attn_Wv"])
    d["bq"] = np.ascontiguousarray(inp["L0_attn_bq"].T)
    d["bk"] = np.ascontiguousarray(inp["L0_attn_bk"].T)
    d["bv"] = np.ascontiguousarray(inp["L0_attn_bv"].T)
    d["gate_wT"] = np.ascontiguousarray(inp["L0_ffn_gate_weight"].T)
    gu = inp["L0_ffn_gate_up_proj"]
    d["wguT"] = np.ascontiguousarray(
        gu.transpose(0, 2, 1).reshape(32, E, 12, 128).transpose(0, 2, 1, 3))
    d["wdT"] = np.ascontiguousarray(inp["L0_ffn_down_proj"].transpose(0, 2, 1))

    wqkvz = inp["L1_attn_W_qkvz"]
    cols = np.arange(12288).reshape(16, 768)
    qcols = cols[:, 0:128].reshape(-1)
    kcols = cols[:, 128:256].reshape(-1)
    vcols = cols[:, 256:512].reshape(-1)
    zcols = cols[:, 512:768].reshape(-1)
    wqkv = wqkvz[:, np.concatenate([qcols, kcols, vcols])]
    d["wqkv"] = np.ascontiguousarray(wqkv.reshape(E, 64, 128).transpose(1, 0, 2))
    d["wz"] = np.ascontiguousarray(wqkvz[:, zcols])
    bacols = np.arange(64).reshape(16, 4)
    d["wba"] = np.ascontiguousarray(
        inp["L1_attn_W_ba"][:, np.concatenate(
            [bacols[:, 0:2].reshape(-1), bacols[:, 2:4].reshape(-1)])])
    d["convw"] = np.ascontiguousarray(inp["L1_attn_conv_weight"])
    d["alog"] = np.ascontiguousarray(inp["L1_attn_A_log"].reshape(32, 1))
    d["dtb"] = np.ascontiguousarray(inp["L1_attn_dt_bias"].reshape(32, 1))
    d["normw"] = np.ascontiguousarray(
        np.tile(inp["L1_attn_norm_weight"], 32).reshape(1, 4096))
    d["wout"] = np.ascontiguousarray(inp["L1_attn_W_out"])

    d["gup"] = np.ascontiguousarray(inp["L1_ffn_gate_up_packed"])
    d["gus"] = np.ascontiguousarray(inp["L1_ffn_gate_up_scales"].T)
    d["dnp"] = np.ascontiguousarray(inp["L1_ffn_down_packed"])
    d["dns"] = np.ascontiguousarray(inp["L1_ffn_down_scales"].T)

    d["ident"] = np.eye(128, dtype=BF)
    d["identf"] = np.eye(128, dtype=np.float32)
    d["onesf"] = np.ones((1, 128), dtype=np.float32)
    cmask = np.where(np.arange(128)[:, None] >= np.arange(128)[None, :], 0.0, -1e30)
    d["cmask"] = cmask.astype(np.float32)
    d["triu64"] = np.triu(np.ones((64, 64), dtype=np.float32))
    bd = np.zeros((128, 128), dtype=np.int32)
    lo = np.tril(np.ones((64, 64), dtype=np.int32), -1)
    bd[0:64, 0:64] = lo
    bd[64:128, 64:128] = lo
    d["bdlow"] = bd
    bdi = np.zeros((128, 128), dtype=np.int32)
    loi = np.tril(np.ones((64, 64), dtype=np.int32))
    bdi[0:64, 0:64] = loi
    bdi[64:128, 64:128] = loi
    d["bdlowi"] = bdi
    d["bdupi"] = np.ascontiguousarray(bdi.T)
    m = np.tril(np.ones((64, 64), dtype=np.int32))
    d["m64low"] = np.ascontiguousarray(np.vstack([m, m]))
    d["m64up"] = np.ascontiguousarray(np.vstack([m.T, m.T]))
    i64 = np.eye(64, dtype=np.float32)
    d["identf2"] = np.ascontiguousarray(np.vstack([i64, i64]))
    return d


def sbuf_all(ctx, nc, specs):
    out = {}
    for name, shape, dt in specs:
        out[name] = ctx.enter_context(nc.sbuf_tensor(name, shape, dt))
    return out


def emit_ln_tile(em, b, st, src_tok, x_tile, dst_lnT, lnT_war=None):
    prev = b.get("ln_prev")
    v1 = em.op("vector", lambda V: V.tensor_reduce(
        out=b["ln_s1"][:], in_=x_tile, axis=AX.X, op=ALU.add), deps=[src_tok, prev])
    a1 = em.op("scalar", lambda A: A.activation(
        out=b["ln_bf"][:], in_=x_tile, func=AF.Square, accum_out=b["ln_ssq"][:]),
        deps=[src_tok, prev, b.get("ln_t1last")])
    v2 = em.op("vector", lambda V: V.tensor_scalar(
        out=b["ln_mean"][:], in0=b["ln_s1"][:], scalar1=1.0 / E, scalar2=None,
        op0=ALU.mult), deps=[v1])
    v3 = em.op("vector", lambda V: V.tensor_tensor(
        out=b["ln_msq"][:], in0=b["ln_mean"][:], in1=b["ln_mean"][:], op=ALU.mult),
        deps=[v2])
    v4 = em.op("vector", lambda V: V.scalar_tensor_tensor(
        out=b["ln_var"][:], in0=b["ln_ssq"][:], scalar=1.0 / E, in1=b["ln_msq"][:],
        op0=ALU.mult, op1=ALU.subtract), deps=[a1, v3])
    v5 = em.op("vector", lambda V: V.tensor_scalar(
        out=b["ln_var"][:], in0=b["ln_var"][:], scalar1=1e-5, scalar2=None,
        op0=ALU.add), deps=[v4])
    a2 = em.op("scalar", lambda A: A.activation(
        out=b["ln_rstd"][:], in_=b["ln_var"][:], func=AF.Ln), deps=[v5])
    a3 = em.op("scalar", lambda A: A.activation(
        out=b["ln_rstd"][:], in_=b["ln_rstd"][:], func=AF.Exp, scale=-0.5), deps=[a2])
    v6 = em.op("vector", lambda V: V.tensor_scalar(
        out=b["ln_bf"][:], in0=x_tile, scalar1=b["ln_mean"][:], scalar2=b["ln_rstd"][:],
        op0=ALU.subtract, op1=ALU.mult), deps=[v2, a3, b.get("ln_t1last")])
    b["ln_prev"] = v6
    b["ln_a1"] = a1
    wtoks = []
    for c in range(EC):
        pp = b["tp_ps"][c % 2]
        t1 = em.op("tensor", lambda T, c=c, pp=pp: T.transpose(
            pp[:, 0:128], b["ln_bf"][:, c * 128:(c + 1) * 128], b["ident"][:]),
            deps=[v6, b["ident_tok"], b["tp_free"][c % 2], lnT_war])
        eng = "scalar" if (c % 2 == 0) else "vector"
        if eng == "scalar":
            fn = (lambda A, c=c, pp=pp: A.copy(
                out=dst_lnT[:, c, st * 128:(st + 1) * 128], in_=pp[:, 0:128]))
        else:
            fn = (lambda V, c=c, pp=pp: V.tensor_copy(
                out=dst_lnT[:, c, st * 128:(st + 1) * 128], in_=pp[:, 0:128]))
        t2 = em.op(eng, fn, deps=[t1])
        b["tp_free"][c % 2] = t2
        b["ln_t1last"] = t1
        wtoks.append(t2)
    return wtoks, v6


def build_phase1(nc, dr):
    with ExitStack() as ctx:
        sb = sbuf_all(ctx, nc, [
            ("xlnT", [128, EC, S], BF16),
            ("attn_out", [128, NT, E], BF16),
            ("ident", [128, 128], BF16),
            ("cmask", [128, 128], F32),
            ("bq", [128, 16], F32), ("bk", [128, 4], F32), ("bv", [128, 4], F32),
            ("wq0", [128, EC, 128], BF16), ("wq1", [128, EC, 128], BF16),
            ("wkv0", [128, EC, 256], BF16), ("wkv1", [128, EC, 256], BF16),
            ("QT", [128, S], BF16), ("KT", [128, S], BF16), ("VT", [128, S], BF16),
            ("Vsb", [128, NT, 128], BF16),
            ("Psb", [128, S], BF16), ("PT", [128, NT, 128], BF16),
            ("negmax", [128, 1], F32), ("denom", [128, 1], F32), ("dinv", [128, 1], F32),
            ("xt0", [128, E], F32), ("xt1", [128, E], F32),
            ("r_new", [128, E], F32),
            ("ln_s1", [128, 1], F32), ("ln_ssq", [128, 1], F32),
            ("ln_mean", [128, 1], F32), ("ln_msq", [128, 1], F32),
            ("ln_var", [128, 1], F32), ("ln_rstd", [128, 1], F32),
            ("ln_bf", [128, E], BF16),
            ("gwT", [128, EC, 32], BF16),
            ("logits", [128, 32], F32), ("probs", [128, 32], F32), ("pwork", [128, 32], F32),
            ("tv", [128, 8], F32), ("cum", [128, 8], F32), ("dsub", [128, 8], F32),
            ("actf", [128, 8], F32), ("acti", [128, 8], I32),
            ("tvm", [128, 8], F32), ("big8", [128, 8], F32),
            ("msk32", [128, 32], F32),
            ("s8", [128, 1], F32), ("thr", [128, 1], F32), ("tau", [128, 1], F32),
            ("rden", [128, 1], F32), ("rdinv", [128, 1], F32),
            ("ce_t", [128, 32], F32),
        ])
        ps_S = ctx.enter_context(nc.psum_tensor("ps_S", [128, 2048], F32))
        ps_t0 = ctx.enter_context(nc.psum_tensor("ps_t0", [128, 1024], BF16))
        ps_t1 = ctx.enter_context(nc.psum_tensor("ps_t1", [128, 1024], BF16))
        ps_o = ctx.enter_context(nc.psum_tensor("ps_o", [128, 512], F32))
        ps_r = ctx.enter_context(nc.psum_tensor("ps_r", [128, 512], F32))
        block = ctx.enter_context(nc.Block())
        em = Em(nc, block, ctx)
        b = dict(sb)
        b["tp_ps"] = [ps_t0, ps_t1]
        b["tp_free"] = [None, None]

        tk_id = em.dma("sync", sb["ident"][:], dr["ident"])
        b["ident_tok"] = tk_id
        tk_cm = em.dma("sync", sb["cmask"][:], dr["cmask"])
        tk_bq = em.dma("sync", sb["bq"][:], dr["bq"])
        tk_bk = em.dma("sync", sb["bk"][:], dr["bk"])
        tk_bv = em.dma("sync", sb["bv"][:], dr["bv"])
        tk_gw = em.dma("gpsimd", sb["gwT"][:],
                       dr["gate_wT"].rearrange("(c p) n -> p c n", p=128))

        xload = [None, None]
        ln_w = []
        for st in range(NT):
            xt = sb["xt0"] if st % 2 == 0 else sb["xt1"]
            tl = em.dma("sync", xt[:], dr["x"][st * 128:(st + 1) * 128, :],
                        deps=xload[st % 2] or [])
            wt, v6 = emit_ln_tile(em, b, st, tl, xt[:], sb["xlnT"])
            xload[st % 2] = [v6, b["ln_a1"]]
            ln_w += wt
        xln_all = ln_w + [tk_id]

        H = 16
        wqb = [sb["wq0"], sb["wq1"]]
        wkvb = [sb["wkv0"], sb["wkv1"]]
        wq_tok = [None, None]
        wkv_tok = [None, None]
        wq_free = [None, None]
        wkv_free = [None, None]

        def load_wq(h):
            i = h % 2
            wq_tok[i] = em.dma("gpsimd", wqb[i][:],
                               dr["wq"][h].rearrange("(c p) d -> p c d", p=128),
                               deps=[wq_free[i]])

        def load_wkv(g):
            i = g % 2
            t1 = em.dma("gpsimd", wkvb[i][:, :, 0:128],
                        dr["wk"][g].rearrange("(c p) d -> p c d", p=128),
                        deps=[wkv_free[i]])
            t2 = em.dma("gpsimd", wkvb[i][:, :, 128:256],
                        dr["wv"][g].rearrange("(c p) d -> p c d", p=128),
                        deps=[wkv_free[i]])
            wkv_tok[i] = [t1, t2]

        load_wq(0)
        load_wkv(0)

        prev_exp_last = None
        kv_ready = None
        smm_last_head = None

        for h in range(H):
            g = h // 4
            wi, ki = h % 2, g % 2
            if h + 1 < H:
                load_wq(h + 1)
            if h % 4 == 0 and g + 1 < 4:
                load_wkv(g + 1)

            proj_toks = []
            for j in range(4):
                mt = None
                for c in range(EC):
                    dep0 = [wq_tok[wi], prev_exp_last, smm_last_head] + xln_all \
                        if c == 0 else []
                    mt = em.op("tensor", lambda T, j=j, c=c, wi=wi: T.matmul(
                        ps_S[:, j * 512:(j + 1) * 512], wqb[wi][:, c, :],
                        sb["xlnT"][:, c, j * 512:(j + 1) * 512],
                        start=(c == 0), stop=(c == EC - 1)),
                        deps=dep0, inc=(c == EC - 1))
                ev = em.op("vector", lambda V, j=j, h=h: V.tensor_scalar(
                    out=sb["QT"][:, j * 512:(j + 1) * 512],
                    in0=ps_S[:, j * 512:(j + 1) * 512],
                    scalar1=sb["bq"][:, h:h + 1], scalar2=None, op0=ALU.add),
                    deps=[mt, tk_bq])
                proj_toks.append(ev)
            wq_free[wi] = proj_toks[-1]

            if h % 4 == 0:
                kv_toks = []
                for j in range(4):
                    mt = None
                    for c in range(EC):
                        dep0 = [wkv_tok[ki], proj_toks[j]] if c == 0 else []
                        mt = em.op("tensor", lambda T, j=j, c=c, ki=ki: T.matmul(
                            ps_S[:, j * 512:(j + 1) * 512], wkvb[ki][:, c, 0:128],
                            sb["xlnT"][:, c, j * 512:(j + 1) * 512],
                            start=(c == 0), stop=(c == EC - 1)),
                            deps=dep0, inc=(c == EC - 1))
                    ev = em.op("scalar", lambda A, j=j, g=g: A.activation(
                        out=sb["KT"][:, j * 512:(j + 1) * 512],
                        in_=ps_S[:, j * 512:(j + 1) * 512],
                        func=AF.Identity, bias=sb["bk"][:, g:g + 1]), deps=[mt, tk_bk])
                    kv_toks.append(ev)
                vt_toks = []
                for j in range(4):
                    mt = None
                    for c in range(EC):
                        dep0 = [kv_toks[j]] if c == 0 else []
                        mt = em.op("tensor", lambda T, j=j, c=c, ki=ki: T.matmul(
                            ps_S[:, j * 512:(j + 1) * 512], wkvb[ki][:, c, 128:256],
                            sb["xlnT"][:, c, j * 512:(j + 1) * 512],
                            start=(c == 0), stop=(c == EC - 1)),
                            deps=dep0, inc=(c == EC - 1))
                    ev = em.op("scalar", lambda A, j=j, g=g: A.activation(
                        out=sb["VT"][:, j * 512:(j + 1) * 512],
                        in_=ps_S[:, j * 512:(j + 1) * 512],
                        func=AF.Identity, bias=sb["bv"][:, g:g + 1]), deps=[mt, tk_bv])
                    vt_toks.append(ev)
                wkv_free[ki] = vt_toks[-1]
                vtr = []
                o_last = b.get("omm_last")
                for stt in range(NT):
                    pp = b["tp_ps"][stt % 2]
                    t1 = em.op("tensor", lambda T, stt=stt, pp=pp: T.transpose(
                        pp[:, 0:128], sb["VT"][:, stt * 128:(stt + 1) * 128],
                        sb["ident"][:]),
                        deps=[vt_toks[stt // 4], tk_id, b["tp_free"][stt % 2]])
                    t2 = em.op("vector", lambda V, stt=stt, pp=pp: V.tensor_copy(
                        out=sb["Vsb"][:, stt, :], in_=pp[:, 0:128]),
                        deps=[t1, o_last])
                    b["tp_free"][stt % 2] = t2
                    vtr.append(t2)
                kv_ready = (vtr, kv_toks)
                psS_free = vt_toks + proj_toks
            else:
                psS_free = proj_toks
            vtr, kv_toks = kv_ready

            aexp_prev = None
            pt_t1_last = None
            for qt in range(NT):
                nk = qt + 1
                smm = []
                for j in range((nk * 128 + 511) // 512):
                    n0, n1 = j * 512, min(nk * 128, (j + 1) * 512)
                    dep0 = []
                    if j == 0:
                        dep0 = [aexp_prev if aexp_prev is not None else prev_exp_last]
                        if qt == 0:
                            dep0 += psS_free + kv_toks
                    t = em.op("tensor", lambda T, qt=qt, n0=n0, n1=n1: T.matmul(
                        ps_S[:, n0:n1], sb["QT"][:, qt * 128:(qt + 1) * 128],
                        sb["KT"][:, n0:n1], start=True, stop=True),
                        deps=dep0, inc=True)
                    smm.append(t)
                vmask = em.op("vector", lambda V, qt=qt: V.tensor_tensor(
                    out=ps_S[:, qt * 128:(qt + 1) * 128],
                    in0=ps_S[:, qt * 128:(qt + 1) * 128],
                    in1=sb["cmask"][:], op=ALU.add), deps=smm + [tk_cm])
                vmax = em.op("vector", lambda V, nk=nk: V.tensor_reduce(
                    out=sb["negmax"][:], in_=ps_S[:, 0:nk * 128], axis=AX.X,
                    op=ALU.max, negate=True), deps=[vmask])
                vmaxs = em.op("vector", lambda V: V.tensor_scalar(
                    out=sb["negmax"][:], in0=sb["negmax"][:], scalar1=ISCALE,
                    scalar2=None, op0=ALU.mult), deps=[vmax])
                aexp = em.op("scalar", lambda A, nk=nk: A.activation(
                    out=sb["Psb"][:, 0:nk * 128], in_=ps_S[:, 0:nk * 128],
                    func=AF.Exp, bias=sb["negmax"][:], scale=ISCALE,
                    accum_out=sb["denom"][:]), deps=[vmaxs, pt_t1_last])
                vdinv = em.op("vector", lambda V: V.reciprocal(
                    out=sb["dinv"][:], in_=sb["denom"][:]), deps=[aexp])
                pt_toks = []
                for kt in range(nk):
                    pp = b["tp_ps"][kt % 2]
                    t1 = em.op("tensor", lambda T, kt=kt, pp=pp: T.transpose(
                        pp[:, 0:128], sb["Psb"][:, kt * 128:(kt + 1) * 128],
                        sb["ident"][:]),
                        deps=[aexp, b["tp_free"][kt % 2]])
                    pt_t1_last = t1
                    eng = "scalar" if kt % 2 == 0 else "vector"
                    if eng == "scalar":
                        fn = (lambda A, kt=kt, pp=pp: A.copy(
                            out=sb["PT"][:, kt, :], in_=pp[:, 0:128]))
                    else:
                        fn = (lambda V, kt=kt, pp=pp: V.tensor_copy(
                            out=sb["PT"][:, kt, :], in_=pp[:, 0:128]))
                    t2 = em.op(eng, fn, deps=[t1, b.get("omm_last")])
                    b["tp_free"][kt % 2] = t2
                    pt_toks.append(t2)
                omm = None
                for kt in range(nk):
                    dep0 = [pt_toks[kt], vtr[kt]]
                    if kt == 0:
                        dep0.append(b.get("oe_last"))
                    omm = em.op("tensor", lambda T, kt=kt, nk=nk: T.matmul(
                        ps_o[:, 0:128], sb["PT"][:, kt, :], sb["Vsb"][:, kt, :],
                        start=(kt == 0), stop=(kt == nk - 1)),
                        deps=dep0, inc=(kt == nk - 1))
                b["omm_last"] = omm
                oe = em.op("vector", lambda V, h=h, qt=qt: V.tensor_scalar(
                    out=sb["attn_out"][:, qt, h * 128:(h + 1) * 128],
                    in0=ps_o[:, 0:128], scalar1=sb["dinv"][:], scalar2=None,
                    op0=ALU.mult), deps=[omm, vdinv])
                b["oe_last"] = oe
                b[("oe", h, qt)] = oe
                aexp_prev = aexp
                if qt == NT - 1:
                    prev_exp_last = aexp
                    smm_last_head = smm[-1]

        em.barrier()
        b["ln_prev"] = None
        b["ln_t1last"] = None
        b["tp_free"] = [None, None]
        xload = [None, None]
        vbg = em.op("vector", lambda V: V.memset(sb["big8"][:], 1e30))
        res_prev = None
        rt_prev = None
        ce_prev = None
        ln_w2 = []
        for st in range(NT):
            xt = sb["xt0"] if st % 2 == 0 else sb["xt1"]
            tl = em.dma("sync", xt[:], dr["x"][st * 128:(st + 1) * 128, :],
                        deps=xload[st % 2] or [])
            radd = em.op("vector", lambda V, st=st, xt=xt: V.tensor_tensor(
                out=sb["r_new"][:], in0=xt[:], in1=sb["attn_out"][:, st, :],
                op=ALU.add), deps=[tl, b[("oe", 15, st)], res_prev,
                                   b.get("ln_prev"), b.get("ln_a1")])
            xload[st % 2] = [radd]
            tso = em.dma("sync", dr["res1"][st * 128:(st + 1) * 128, :],
                         sb["r_new"][:], deps=[radd])
            res_prev = tso
            wt, v6 = emit_ln_tile(em, b, st, radd, sb["r_new"][:], sb["xlnT"])
            ln_w2 += wt
            lg = None
            for c in range(EC):
                dep0 = [wt[c], tk_gw]
                if c == 0:
                    dep0.append(rt_prev)
                lg = em.op("tensor", lambda T, st=st, c=c: T.matmul(
                    ps_r[:, 0:32], sb["xlnT"][:, c, st * 128:(st + 1) * 128],
                    sb["gwT"][:, c, :], start=(c == 0), stop=(c == EC - 1)),
                    deps=dep0, inc=(c == EC - 1))
            vle = em.op("vector", lambda V: V.tensor_copy(
                out=sb["logits"][:], in_=ps_r[:, 0:32]), deps=[lg, ce_prev])
            rt_prev = vle
            vm = em.op("vector", lambda V: V.tensor_reduce(
                out=sb["negmax"][:], in_=sb["logits"][:], axis=AX.X, op=ALU.max,
                negate=True), deps=[vle])
            ae = em.op("scalar", lambda A: A.activation(
                out=sb["probs"][:], in_=sb["logits"][:], func=AF.Exp,
                bias=sb["negmax"][:], accum_out=sb["denom"][:]), deps=[vm])
            vd = em.op("vector", lambda V: V.reciprocal(
                out=sb["dinv"][:], in_=sb["denom"][:]), deps=[ae])
            vp = em.op("vector", lambda V: V.tensor_scalar(
                out=sb["probs"][:], in0=sb["probs"][:], scalar1=sb["dinv"][:],
                scalar2=None, op0=ALU.mult), deps=[vd])
            vw = em.op("vector", lambda V: V.tensor_copy(
                out=sb["pwork"][:], in_=sb["probs"][:]), deps=[vp])
            last = vw
            for rr in range(8):
                m1 = em.op("vector", lambda V, rr=rr: V.tensor_reduce(
                    out=sb["tv"][:, rr:rr + 1], in_=sb["pwork"][:], axis=AX.X,
                    op=ALU.max), deps=[last])
                m2 = em.op("vector", lambda V: V.tensor_scalar(
                    out=sb["msk32"][:], in0=sb["pwork"][:],
                    scalar1=sb["tv"][:, rr:rr + 1], scalar2=None, op0=ALU.is_ge),
                    deps=[m1])
                last = em.op("vector", lambda V: V.scalar_tensor_tensor(
                    out=sb["pwork"][:], in0=sb["msk32"][:], scalar=-1e30,
                    in1=sb["pwork"][:], op0=ALU.mult, op1=ALU.add), deps=[m2])
            v8 = em.op("vector", lambda V: V.tensor_reduce(
                out=sb["s8"][:], in_=sb["tv"][:], axis=AX.X, op=ALU.add), deps=[last])
            vc = em.op("vector", lambda V: V.tensor_tensor_scan(
                out=sb["cum"][:], data0=sb["tv"][:], data1=sb["tv"][:], initial=0.0,
                op0=ALU.add, op1=ALU.bypass), deps=[last])
            vt1 = em.op("vector", lambda V: V.tensor_scalar(
                out=sb["thr"][:], in0=sb["s8"][:], scalar1=0.8, scalar2=None,
                op0=ALU.mult), deps=[v8])
            vt2 = em.op("vector", lambda V: V.tensor_tensor(
                out=sb["dsub"][:], in0=sb["cum"][:], in1=sb["tv"][:],
                op=ALU.subtract), deps=[vc])
            vai = em.op("vector", lambda V: V.tensor_scalar(
                out=sb["acti"][:], in0=sb["dsub"][:], scalar1=sb["thr"][:],
                scalar2=None, op0=ALU.is_lt), deps=[vt1, vt2])
            vaf = em.op("vector", lambda V: V.tensor_scalar(
                out=sb["actf"][:], in0=sb["dsub"][:], scalar1=sb["thr"][:],
                scalar2=None, op0=ALU.is_lt), deps=[vt1, vt2])
            vs = em.op("vector", lambda V: V.select(
                out=sb["tvm"][:], mask=sb["acti"][:], on_true=sb["tv"][:],
                on_false=sb["big8"][:], add_drain=True), deps=[vai, vbg])
            vtau = em.op("vector", lambda V: V.tensor_reduce(
                out=sb["tau"][:], in_=sb["tvm"][:], axis=AX.X, op=ALU.min), deps=[vs])
            vden = em.op("vector", lambda V: V.tensor_tensor(
                out=sb["tvm"][:], in0=sb["tv"][:], in1=sb["actf"][:], op=ALU.mult),
                deps=[vaf, vtau])
            vden2 = em.op("vector", lambda V: V.tensor_reduce(
                out=sb["rden"][:], in_=sb["tvm"][:], axis=AX.X, op=ALU.add),
                deps=[vden])
            vri = em.op("vector", lambda V: V.reciprocal(
                out=sb["rdinv"][:], in_=sb["rden"][:]), deps=[vden2])
            vm2 = em.op("vector", lambda V: V.tensor_scalar(
                out=sb["msk32"][:], in0=sb["probs"][:], scalar1=sb["tau"][:],
                scalar2=None, op0=ALU.is_ge), deps=[vtau, vri])
            vce1 = em.op("vector", lambda V: V.tensor_tensor(
                out=sb["ce_t"][:], in0=sb["probs"][:], in1=sb["msk32"][:],
                op=ALU.mult), deps=[vm2, ce_prev])
            vce = em.op("vector", lambda V: V.tensor_scalar(
                out=sb["ce_t"][:], in0=sb["ce_t"][:], scalar1=sb["rdinv"][:],
                scalar2=None, op0=ALU.mult), deps=[vce1])
            ce_prev = em.dma("sync", dr["ce"][st * 128:(st + 1) * 128, :],
                             sb["ce_t"][:], deps=[vce])
        tx = em.dma("gpsimd", dr["xln2T"], sb["xlnT"][:], deps=ln_w2)
        em.barrier()
    return nc


NPDT = {F32: np.float32, BF16: BF, F16: np.float16, I32: np.int32}

PH1_INS = [("x", F32, "x", True), ("wq", F32, "wq", False), ("wk", F32, "wk", False),
           ("wv", F32, "wv", False), ("bq", F32, "bq", False), ("bk", F32, "bk", False),
           ("bv", F32, "bv", False), ("gate_wT", F32, "gate_wT", False),
           ("ident", BF16, "ident", False), ("cmask", F32, "cmask", False)]


def run_phase1(hp):
    nc = bass.Bass("TRN2", target_bir_lowering=False, debug=False)
    dr = {}
    for name, dt, key, percore in PH1_INS:
        arr = hp[key][0] if percore else hp[key]
        dr[name] = nc.dram_tensor("in_" + name, list(arr.shape), dt, kind="ExternalInput").ap()
    dr["res1"] = nc.dram_tensor("res1", [S, E], F32, kind="ExternalOutput").ap()
    dr["xln2T"] = nc.dram_tensor("xln2T", [128, EC, S], BF16, kind="ExternalOutput").ap()
    dr["ce"] = nc.dram_tensor("ce", [S, 32], F32, kind="ExternalOutput").ap()
    build_phase1(nc, dr)
    in_maps = []
    for c in range(NCORES):
        m = {}
        for name, dt, key, percore in PH1_INS:
            arr = hp[key][c] if percore else hp[key]
            m["in_" + name] = arr
        in_maps.append(m)
    res = run_bass_kernel_spmd(nc, in_maps, core_ids=list(range(NCORES)))
    return res.results


def kernel(**inputs):
    hp = host_prep(inputs)
    r1 = run_phase1(hp)
    r2 = run_phase2(hp, r1)
    r2b = run_phase2b(hp)
    r3 = run_phase3(hp, r2)
    res3 = [r3[c]["res3"] for c in range(NCORES)]
    xln4T = [r3[c]["xln4T"] for c in range(NCORES)]
    r4 = run_phase4(hp, r2b, res3, xln4T)
    out = np.stack([r["out"] for r in r4]).astype(np.float32)
    return out


def build_phase2(nc, dr):
    NE = 32
    with ExitStack() as ctx:
        sb = sbuf_all(ctx, nc, [
            ("xq", [128, EC, 1024], BF16),
            ("acc", [128, 8, E], F32),
            ("hT", [128, 6, 1024], BF16),
            ("sg0", [128, 512], F32), ("sg1", [128, 512], F32),
            ("ceh", [128, 8, 32], F32),
            ("wgu0", [128, EC, 128], BF16), ("wgu1", [128, EC, 128], BF16),
            ("wgu2", [128, EC, 128], BF16), ("wgu3", [128, EC, 128], BF16),
            ("wd", [128, 6, E], BF16),
            ("ident", [128, 128], BF16),
            ("lnstage", [128, EC, 128], BF16),
            ("ln_s1", [128, 1], F32), ("ln_ssq", [128, 1], F32),
            ("ln_mean", [128, 1], F32), ("ln_msq", [128, 1], F32),
            ("ln_var", [128, 1], F32), ("ln_rstd", [128, 1], F32),
            ("ln_bf", [128, E], BF16),
        ])
        ps_g = [ctx.enter_context(nc.psum_tensor(f"ps_g{i}", [128, 512], F32)) for i in range(2)]
        ps_u = [ctx.enter_context(nc.psum_tensor(f"ps_u{i}", [128, 512], F32)) for i in range(2)]
        ps_d = [ctx.enter_context(nc.psum_tensor(f"ps_d{i}", [128, 512], F32)) for i in range(2)]
        ps_t0 = ctx.enter_context(nc.psum_tensor("ps_t0", [128, 1024], BF16))
        ps_t1 = ctx.enter_context(nc.psum_tensor("ps_t1", [128, 1024], BF16))
        block = ctx.enter_context(nc.Block())
        em = Em(nc, block, ctx)
        b = dict(sb)
        b["tp_ps"] = [ps_t0, ps_t1]
        b["tp_free"] = [None, None]
        b["ident_tok"] = em.dma("sync", sb["ident"][:], dr["ident"])

        wgub = [sb["wgu0"], sb["wgu1"], sb["wgu2"], sb["wgu3"]]
        wgu_tok = [None] * 4
        freepair = [None, None]
        xq_war, acc_war, ce_war = [], [], []
        wd_tok, wd_free = None, None
        psg_free = [None, None]
        psu_free = [None, None]
        psd_free = [None, None]

        def load_pair(e, pr):
            slot = (pr % 2) * 2
            dep = [freepair[pr % 2]]
            wgu_tok[slot] = em.dma("gpsimd", wgub[slot][:],
                                   dr["wguT"][e, pr].rearrange("(c p) o -> p c o", p=128),
                                   deps=dep)
            wgu_tok[slot + 1] = em.dma("gpsimd", wgub[slot + 1][:],
                                       dr["wguT"][e, pr + 6].rearrange("(c p) o -> p c o", p=128),
                                       deps=dep)

        first = True
        for hf in range(2):
            s0 = hf * 1024
            tk_xq = em.dma("sync", sb["xq"][:], dr["xln2T"][:, :, s0:s0 + 1024],
                           deps=xq_war)
            tk_acc = [em.dma("sync", sb["acc"][:, t, :],
                             dr["res1"][s0 + t * 128:s0 + (t + 1) * 128, :],
                             deps=acc_war) for t in range(8)]
            tk_ce = em.dma("sync", sb["ceh"][:],
                           dr["ce"].rearrange("(q t p) n -> q p t n", p=128, t=8)[hf],
                           deps=ce_war)
            if first:
                load_pair(0, 0)
                wd_tok = em.dma("gpsimd", sb["wd"][:],
                                dr["wdT"][0].rearrange("(c p) o -> p c o", p=128),
                                deps=[wd_free])
                first = False
            acc_evac = {}
            hv_last = None
            down_mm_last = b.get("p2_down_last")
            for e in range(NE):
                gu_last = None
                for pr in range(6):
                    slot = (pr % 2) * 2
                    pg, pu = ps_g[pr % 2], ps_u[pr % 2]
                    for sc in range(2):
                        gmm = None
                        for c in range(EC):
                            dep0 = [wgu_tok[slot], tk_xq, psg_free[pr % 2]] if c == 0 else []
                            gmm = em.op("tensor", lambda T, c=c, sc=sc, slot=slot, pg=pg: T.matmul(
                                pg[:], wgub[slot][:, c, :],
                                sb["xq"][:, c, sc * 512:(sc + 1) * 512],
                                start=(c == 0), stop=(c == EC - 1)),
                                deps=dep0, inc=(c == EC - 1))
                        umm = None
                        for c in range(EC):
                            dep0 = [wgu_tok[slot + 1], psu_free[pr % 2]] if c == 0 else []
                            umm = em.op("tensor", lambda T, c=c, sc=sc, slot=slot, pu=pu: T.matmul(
                                pu[:], wgub[slot + 1][:, c, :],
                                sb["xq"][:, c, sc * 512:(sc + 1) * 512],
                                start=(c == 0), stop=(c == EC - 1)),
                                deps=dep0, inc=(c == EC - 1))
                        gu_last = umm
                        sgb = sb["sg0"] if sc == 0 else sb["sg1"]
                        sa = em.op("scalar", lambda A, pg=pg, sgb=sgb: A.activation(
                            out=sgb[:], in_=pg[:], func=AF.Silu),
                            deps=[gmm, b.get(("sgfree", sc))])
                        psg_free[pr % 2] = sa
                        hv = em.op("vector", lambda V, pr=pr, sc=sc, pu=pu, sgb=sgb: V.tensor_tensor(
                            out=sb["hT"][:, pr, sc * 512:(sc + 1) * 512],
                            in0=sgb[:], in1=pu[:], op=ALU.mult),
                            deps=[sa, umm, down_mm_last])
                        psu_free[pr % 2] = hv
                        b[("sgfree", sc)] = hv
                        hv_last = hv
                    freepair[pr % 2] = gu_last
                    if pr < 5:
                        load_pair(e, pr + 1)
                    elif e + 1 < NE:
                        load_pair(e + 1, 0)
                    elif hf == 0:
                        load_pair(0, 0)
                dmm_final = None
                for t in range(8):
                    for oc in range(4):
                        pd = ps_d[(t * 4 + oc) % 2]
                        dmm = None
                        for c in range(6):
                            dep0 = [hv_last, wd_tok, psd_free[(t * 4 + oc) % 2]] if c == 0 else []
                            dmm = em.op("tensor", lambda T, t=t, oc=oc, c=c, pd=pd: T.matmul(
                                pd[:], sb["hT"][:, c, t * 128:(t + 1) * 128],
                                sb["wd"][:, c, oc * 512:(oc + 1) * 512],
                                start=(c == 0), stop=(c == 5)),
                                deps=dep0, inc=(c == 5))
                        st_ = em.op("vector", lambda V, t=t, oc=oc, e=e, pd=pd: V.scalar_tensor_tensor(
                            out=sb["acc"][:, t, oc * 512:(oc + 1) * 512], in0=pd[:],
                            scalar=sb["ceh"][:, t, e:e + 1],
                            in1=sb["acc"][:, t, oc * 512:(oc + 1) * 512],
                            op0=ALU.mult, op1=ALU.add),
                            deps=[dmm, tk_ce, tk_acc[t], acc_evac.get((t, oc))])
                        acc_evac[(t, oc)] = st_
                        psd_free[(t * 4 + oc) % 2] = st_
                        dmm_final = dmm
                down_mm_last = dmm_final
                b["p2_down_last"] = dmm_final
                if e + 1 < NE:
                    wd_free = dmm_final
                    wd_tok = em.dma("gpsimd", sb["wd"][:],
                                    dr["wdT"][e + 1].rearrange("(c p) o -> p c o", p=128),
                                    deps=[wd_free])
                elif hf == 0:
                    wd_free = dmm_final
                    wd_tok = em.dma("gpsimd", sb["wd"][:],
                                    dr["wdT"][0].rearrange("(c p) o -> p c o", p=128),
                                    deps=[wd_free])
            b["ln_prev"] = None
            stage_dma = None
            lnv6 = []
            for t in range(8):
                accv = [acc_evac[(t, oc)] for oc in range(4)]
                tso = em.dma("sync", dr["res2"][s0 + t * 128:s0 + (t + 1) * 128, :],
                             sb["acc"][:, t, :], deps=accv)
                wt, v6 = emit_ln_tile(em, b, 0, accv[-1], sb["acc"][:, t, :],
                                      sb["lnstage"], lnT_war=stage_dma)
                em.wait("vector", accv)
                stage_dma = em.dma("sync", dr["xln3T"][:, :, s0 + t * 128:s0 + (t + 1) * 128],
                                   sb["lnstage"][:], deps=wt + [tso])
                lnv6.append(v6)
                lnv6.append(b["ln_a1"])
            xq_war = [gu_last]
            acc_war = lnv6 + [stage_dma]
            ce_war = [acc_evac[(7, 3)]]
        em.barrier()
    return nc


def build_phase2b(nc, dr):
    with ExitStack() as ctx:
        sb = sbuf_all(ctx, nc, [
            ("pk0", [128, 704], I32), ("pk1", [128, 704], I32),
            ("nib", [128, 5632], I32),
            ("t1", [128, 5632], I32),
            ("t2", [128, 5632], I32),
            ("t3", [128, 5632], I32),
            ("wsc", [128, 5632], BF16),
            ("sc0", [128, 44], F32), ("sc1", [128, 44], F32),
            ("stage", [128, 44, 128], BF16),
            ("ident", [128, 128], BF16),
        ])
        ps_t = [ctx.enter_context(nc.psum_tensor(f"ps_t{i}", [128, 1024], BF16))
                for i in range(2)]
        block = ctx.enter_context(nc.Block())
        em = Em(nc, block, ctx)
        tk_id = em.dma("sync", sb["ident"][:], dr["ident"])
        pkb = [sb["pk0"], sb["pk1"]]
        scb = [sb["sc0"], sb["sc1"]]
        pk_tok = [None, None]
        sc_tok = [None, None]
        pk_free = [None, None]
        chain = {}

        def do_tile(idx, kind, t):
            IC = 16 if kind == "gu" else 44
            W = IC * 16
            N = IC * 128
            sl = idx % 2
            src = dr["gup"] if kind == "gu" else dr["dnp"]
            ssrc = dr["gus"] if kind == "gu" else dr["dns"]
            pk_tok[sl] = em.dma("sync", pkb[sl][:, 0:W], src[t * 128:(t + 1) * 128, :],
                                deps=[pk_free[sl]])
            sc_tok[sl] = em.dma("gpsimd", scb[sl][:, 0:IC], ssrc[t * 128:(t + 1) * 128, :],
                                deps=[pk_free[sl]])
            toks = []
            for j in range(8):
                eng = "vector"
                toks.append(em.op(eng, lambda X, j=j, sl=sl, N=N, W=W: X.tensor_scalar(
                    out=sb["nib"][:, j:N:8], in0=pkb[sl][:, 0:W], scalar1=4 * j, scalar2=15,
                    op0=ALU.logical_shift_right, op1=ALU.bitwise_and),
                    deps=[pk_tok[sl], chain.get("c0"), chain.get("c0g")]))
            v1 = em.op("vector", lambda V, N=N: V.tensor_single_scalar(
                out=sb["t1"][:, 0:N], in_=sb["nib"][:, 0:N], scalar=7, op=ALU.bitwise_and),
                deps=toks + [chain.get("t1w")])
            g1 = em.op("vector", lambda G, N=N: G.tensor_scalar(
                out=sb["t2"][:, 0:N], in0=sb["nib"][:, 0:N], scalar1=3, scalar2=31,
                op0=ALU.logical_shift_right, op1=ALU.logical_shift_left),
                deps=toks + [chain.get("t2w")])
            chain["c0"] = v1
            chain["c0g"] = g1
            v2 = em.op("vector", lambda V, N=N: V.tensor_single_scalar(
                out=sb["t3"][:, 0:N], in_=sb["t1"][:, 0:N], scalar=252, op=ALU.add),
                deps=[v1, chain.get("t3w")])
            v3 = em.op("vector", lambda V, N=N: V.scalar_tensor_tensor(
                out=sb["t3"][:, 0:N], in0=sb["t1"][:, 0:N], scalar=1, in1=sb["t3"][:, 0:N],
                op0=ALU.is_ge, op1=ALU.mult), deps=[v2])
            chain["t1w"] = v3
            v4 = em.op("vector", lambda V, N=N: V.tensor_single_scalar(
                out=sb["t3"][:, 0:N], in_=sb["t3"][:, 0:N], scalar=22,
                op=ALU.logical_shift_left), deps=[v3])
            v5 = em.op("vector", lambda V, N=N: V.tensor_tensor(
                out=sb["t3"][:, 0:N], in0=sb["t3"][:, 0:N], in1=sb["t2"][:, 0:N],
                op=ALU.bitwise_or), deps=[v4, g1])
            chain["t2w"] = v5
            v6 = em.op("vector", lambda V, N=N: V.tensor_scalar(
                out=sb["t1"][:, 0:N], in0=sb["t1"][:, 0:N], scalar1=1, scalar2=(1 << 22),
                op0=ALU.is_equal, op1=ALU.mult), deps=[v5])
            v7 = em.op("vector", lambda V, N=N: V.tensor_tensor(
                out=sb["t3"][:, 0:N], in0=sb["t3"][:, 0:N], in1=sb["t1"][:, 0:N],
                op=ALU.subtract), deps=[v6])
            chain["t3w"] = v7
            stoks = []
            for c in range(IC):
                eng = "vector" if c % 2 == 0 else "gpsimd"
                stoks.append(em.op(eng, lambda X, c=c, sl=sl: X.tensor_scalar(
                    out=sb["wsc"][:, c * 128:(c + 1) * 128],
                    in0=sb["t3"][:, c * 128:(c + 1) * 128].bitcast(F32),
                    scalar1=scb[sl][:, c:c + 1], scalar2=None, op0=ALU.mult),
                    deps=[v7, sc_tok[sl], chain.get(("wscw", c))]))
                chain[("wscw", c)] = stoks[-1]
            pk_free[sl] = v1
            pk_free[sl] = toks[-1]
            ttoks = []
            for c in range(IC):
                pp = ps_t[c % 2]
                tt1 = em.op("tensor", lambda T, c=c, pp=pp: T.transpose(
                    pp[:, 0:128], sb["wsc"][:, c * 128:(c + 1) * 128], sb["ident"][:]),
                    deps=[stoks[c], tk_id, chain.get(("tpf", c % 2))])
                eng = "scalar" if c % 2 == 0 else "vector"
                if eng == "scalar":
                    fn = (lambda A, c=c, pp=pp: A.copy(out=sb["stage"][:, c, :], in_=pp[:, 0:128]))
                else:
                    fn = (lambda V, c=c, pp=pp: V.tensor_copy(out=sb["stage"][:, c, :], in_=pp[:, 0:128]))
                tt2 = em.op(eng, fn, deps=[tt1, chain.get("stg_dma")])
                chain[("tpf", c % 2)] = tt2
                ttoks.append(tt2)
            if kind == "gu":
                dmaout = em.dma("sync", dr["wguT_c"][t].rearrange("(c p) o -> p c o", p=128),
                                sb["stage"][:, 0:16, :], deps=ttoks)
            else:
                dmaout = em.dma("sync",
                                dr["wdT_c"][:, :, t * 128:(t + 1) * 128].rearrange("c p o -> p c o"),
                                sb["stage"][:, 0:44, :], deps=ttoks)
            chain["stg_dma"] = dmaout
            for c in range(IC):
                chain[("wscw", c)] = ttoks[c]

        idx = 0
        for t in range(88):
            do_tile(idx, "gu", t); idx += 1
        for t in range(16):
            do_tile(idx, "dn", t); idx += 1
        em.barrier()
    return nc


PH2_INS = [("res1", F32, None, True), ("xln2T", BF16, None, True), ("ce", F32, None, True),
           ("wguT", F32, "wguT", False), ("wdT", F32, "wdT", False),
           ("ident", BF16, "ident", False)]


def run_phase2(hp, r1):
    nc = bass.Bass("TRN2", target_bir_lowering=False, debug=False)
    dr = {}
    shapes = {"res1": [S, E], "xln2T": [128, EC, S], "ce": [S, 32]}
    for name, dt, key, percore in PH2_INS:
        shp = shapes[name] if key is None else list(hp[key].shape)
        dr[name] = nc.dram_tensor("in_" + name, shp, dt, kind="ExternalInput").ap()
    dr["res2"] = nc.dram_tensor("res2", [S, E], F32, kind="ExternalOutput").ap()
    dr["xln3T"] = nc.dram_tensor("xln3T", [128, EC, S], BF16, kind="ExternalOutput").ap()
    build_phase2(nc, dr)
    in_maps = []
    for c in range(NCORES):
        m = {}
        for name, dt, key, percore in PH2_INS:
            m["in_" + name] = r1[c][name] if key is None else hp[key]
        in_maps.append(m)
    res = run_bass_kernel_spmd(nc, in_maps, core_ids=list(range(NCORES)))
    return res.results


def run_phase2b(hp):
    nc = bass.Bass("TRN2", target_bir_lowering=False, debug=False)
    dr = {}
    for name in ("gup", "gus", "dnp", "dns"):
        dt = {"gup": I32, "gus": F16, "dnp": I32, "dns": F16}[name]
        dr[name] = nc.dram_tensor("in_" + name, list(hp[name].shape), dt,
                                  kind="ExternalInput").ap()
    dr["ident"] = nc.dram_tensor("in_ident", [128, 128], BF16, kind="ExternalInput").ap()
    dr["wguT_c"] = nc.dram_tensor("wguT_c", [88, E, 128], BF16, kind="ExternalOutput").ap()
    dr["wdT_c"] = nc.dram_tensor("wdT_c", [44, 128, E], BF16, kind="ExternalOutput").ap()
    build_phase2b(nc, dr)
    in_maps = []
    for c in range(NCORES):
        m = {("in_" + n): hp[n] for n in ("gup", "gus", "dnp", "dns")}
        m["in_ident"] = hp["ident"]
        in_maps.append(m)
    res = run_bass_kernel_spmd(nc, in_maps, core_ids=list(range(NCORES)))
    return res.results


def build_phase4(nc, dr):
    with ExitStack() as ctx:
        sb = sbuf_all(ctx, nc, [
            ("xb", [128, EC, 512], BF16),
            ("hT", [128, 44, 512], BF16),
            ("acc", [128, 4, E], F32),
            ("sg0", [128, 512], F32), ("sg1", [128, 512], F32),
            ("wgu0", [128, EC, 128], BF16), ("wgu1", [128, EC, 128], BF16),
            ("wgu2", [128, EC, 128], BF16), ("wgu3", [128, EC, 128], BF16),
            ("wd0", [128, 8, E], BF16), ("wd1", [128, 8, E], BF16),
            ("xr", [128, E], F32),
            ("ssq", [128, 1], F32), ("rstd", [128, 1], F32),
            ("otile", [128, E], F32),
        ])
        ps_g = [ctx.enter_context(nc.psum_tensor(f"ps_g{i}", [128, 512], F32)) for i in range(2)]
        ps_u = [ctx.enter_context(nc.psum_tensor(f"ps_u{i}", [128, 512], F32)) for i in range(2)]
        ps_d = [ctx.enter_context(nc.psum_tensor(f"ps_d{i}", [128, 512], F32)) for i in range(4)]
        block = ctx.enter_context(nc.Block())
        em = Em(nc, block, ctx)
        b = {}
        wgub = [sb["wgu0"], sb["wgu1"], sb["wgu2"], sb["wgu3"]]
        wdb = [sb["wd0"], sb["wd1"]]
        wgu_tok = [None] * 4
        freepair = [None, None]
        wd_tok = [None, None]
        wd_free = [None, None]
        psg_free = [None, None]
        psu_free = [None, None]
        psd_free = [None] * 4
        xb_war = []
        NG = 6

        def load_pair(t, par):
            slot = par * 2
            dep = [freepair[par]]
            wgu_tok[slot] = em.dma("gpsimd", wgub[slot][:],
                                   dr["wguT_c"][t].rearrange("(c p) o -> p c o", p=128),
                                   deps=dep)
            wgu_tok[slot + 1] = em.dma("gpsimd", wgub[slot + 1][:],
                                       dr["wguT_c"][t + 44].rearrange("(c p) o -> p c o", p=128),
                                       deps=dep)

        def load_wd(g, sl):
            n = min(8, 44 - 8 * g)
            wd_tok[sl] = em.dma("sync", wdb[sl][:, 0:n, :],
                                dr["wdT_c"][8 * g:8 * g + n].rearrange("c p o -> p c o"),
                                deps=[wd_free[sl]])

        load_pair(0, 0)
        load_wd(0, 0)
        acc_evac = {}
        out_dma = {}
        for blk in range(4):
            s0 = blk * 512
            tk_xb = em.dma("sync", sb["xb"][:], dr["xln4T"][:, :, s0:s0 + 512],
                           deps=xb_war)
            hv_last = None
            down_last = b.get("p4_down_last")
            for t in range(44):
                par = t % 2
                slot = par * 2
                pg, pu = ps_g[par], ps_u[par]
                gmm = None
                for c in range(EC):
                    dep0 = [wgu_tok[slot], tk_xb, psg_free[par]] if c == 0 else []
                    gmm = em.op("tensor", lambda T, c=c, slot=slot, pg=pg: T.matmul(
                        pg[:], wgub[slot][:, c, :], sb["xb"][:, c, :],
                        start=(c == 0), stop=(c == EC - 1)), deps=dep0, inc=(c == EC - 1))
                umm = None
                for c in range(EC):
                    dep0 = [wgu_tok[slot + 1], psu_free[par]] if c == 0 else []
                    umm = em.op("tensor", lambda T, c=c, slot=slot, pu=pu: T.matmul(
                        pu[:], wgub[slot + 1][:, c, :], sb["xb"][:, c, :],
                        start=(c == 0), stop=(c == EC - 1)), deps=dep0, inc=(c == EC - 1))
                freepair[par] = umm
                if t + 1 < 44:
                    load_pair(t + 1, (t + 1) % 2)
                elif blk + 1 < 4:
                    load_pair(0, (t + 1) % 2)
                sgb = sb["sg0"] if par == 0 else sb["sg1"]
                sa = em.op("scalar", lambda A, pg=pg, sgb=sgb: A.activation(
                    out=sgb[:], in_=pg[:], func=AF.Silu), deps=[gmm, b.get(("sgf", par))])
                psg_free[par] = sa
                hv = em.op("vector", lambda V, t=t, pu=pu, sgb=sgb: V.tensor_tensor(
                    out=sb["hT"][:, t, :], in0=sgb[:], in1=pu[:], op=ALU.mult),
                    deps=[sa, umm, down_last])
                psu_free[par] = hv
                b[("sgf", par)] = hv
                hv_last = hv
            xb_war = [gmm, umm]
            dmm_all = []
            for g in range(NG):
                n = min(8, 44 - 8 * g)
                sl = g % 2
                for tl in range(4):
                    for oc in range(4):
                        pd = ps_d[oc]
                        dmm = None
                        for c in range(n):
                            dep0 = [hv_last, wd_tok[sl], psd_free[oc]] if c == 0 else []
                            dmm = em.op("tensor", lambda T, g=g, tl=tl, oc=oc, c=c, pd=pd, sl=sl: T.matmul(
                                pd[:], sb["hT"][:, 8 * g + c, tl * 128:(tl + 1) * 128],
                                wdb[sl][:, c, oc * 512:(oc + 1) * 512],
                                start=(c == 0), stop=(c == n - 1)), deps=dep0, inc=(c == n - 1))
                        if g == 0:
                            ev = em.op("vector", lambda V, tl=tl, oc=oc, pd=pd: V.tensor_copy(
                                out=sb["acc"][:, tl, oc * 512:(oc + 1) * 512], in_=pd[:]),
                                deps=[dmm, acc_evac.get((tl, oc)), b.get(("accrd", tl))])
                        else:
                            ev = em.op("vector", lambda V, tl=tl, oc=oc, pd=pd: V.tensor_tensor(
                                out=sb["acc"][:, tl, oc * 512:(oc + 1) * 512], in0=pd[:],
                                in1=sb["acc"][:, tl, oc * 512:(oc + 1) * 512], op=ALU.add),
                                deps=[dmm, acc_evac.get((tl, oc))])
                        acc_evac[(tl, oc)] = ev
                        psd_free[oc] = ev
                        dmm_all.append(dmm)
                wd_free[sl] = dmm_all[-1]
                if g + 1 < NG:
                    load_wd(g + 1, (g + 1) % 2)
                elif blk + 1 < 4:
                    load_wd(0, (g + 1) % 2)
            b["p4_down_last"] = dmm_all[-1]
            for tl in range(4):
                accv = [acc_evac[(tl, oc)] for oc in range(4)]
                a1 = em.op("scalar", lambda A, tl=tl: A.activation(
                    out=sb["otile"][:], in_=sb["acc"][:, tl, :], func=AF.Square,
                    accum_out=sb["ssq"][:]), deps=accv + [b.get("rms_prev"), out_dma.get("o")])
                v1 = em.op("vector", lambda V: V.tensor_scalar(
                    out=sb["ssq"][:], in0=sb["ssq"][:], scalar1=1.0 / E, scalar2=1e-6,
                    op0=ALU.mult, op1=ALU.add), deps=[a1, b.get("rms_prev")])
                a2 = em.op("scalar", lambda A: A.activation(
                    out=sb["rstd"][:], in_=sb["ssq"][:], func=AF.Ln), deps=[v1])
                a3 = em.op("scalar", lambda A: A.activation(
                    out=sb["rstd"][:], in_=sb["rstd"][:], func=AF.Exp, scale=-0.5), deps=[a2])
                tr = em.dma("sync", sb["xr"][:], dr["res3"][s0 + tl * 128:s0 + (tl + 1) * 128, :],
                            deps=[b.get("xr_war")])
                vo = em.op("vector", lambda V, tl=tl: V.scalar_tensor_tensor(
                    out=sb["otile"][:], in0=sb["acc"][:, tl, :], scalar=sb["rstd"][:],
                    in1=sb["xr"][:], op0=ALU.mult, op1=ALU.add),
                    deps=[a3, tr, a1, out_dma.get("o")])
                b["xr_war"] = vo
                b["rms_prev"] = vo
                td = em.dma("sync", dr["out"][s0 + tl * 128:s0 + (tl + 1) * 128, :],
                            sb["otile"][:], deps=[vo])
                out_dma["o"] = td
                for oc in range(4):
                    acc_evac[(tl, oc)] = vo
                b[("accrd", tl)] = a1
        em.barrier()
    return nc


def run_phase4(hp, r2b, res3, xln4T):
    nc = bass.Bass("TRN2", target_bir_lowering=False, debug=False)
    dr = {}
    dr["res3"] = nc.dram_tensor("in_res3", [S, E], F32, kind="ExternalInput").ap()
    dr["xln4T"] = nc.dram_tensor("in_xln4T", [128, EC, S], BF16, kind="ExternalInput").ap()
    dr["wguT_c"] = nc.dram_tensor("in_wguT_c", [88, E, 128], BF16, kind="ExternalInput").ap()
    dr["wdT_c"] = nc.dram_tensor("in_wdT_c", [44, 128, E], BF16, kind="ExternalInput").ap()
    dr["out"] = nc.dram_tensor("out", [S, E], F32, kind="ExternalOutput").ap()
    build_phase4(nc, dr)
    in_maps = []
    for c in range(NCORES):
        in_maps.append({"in_res3": res3[c], "in_xln4T": xln4T[c],
                        "in_wguT_c": r2b[c]["wguT_c"], "in_wdT_c": r2b[c]["wdT_c"]})
    res = run_bass_kernel_spmd(nc, in_maps, core_ids=list(range(NCORES)))
    return res.results


def build_phase3(nc, dr):
    NKH, NVH, DK, DV, CH = 16, 32, 128, 128, 64
    with ExitStack() as ctx:
        sb = sbuf_all(ctx, nc, [
            ("xlnT", [128, EC, S], BF16),
            ("identb", [128, 128], BF16), ("identf", [128, 128], F32),
            ("onesf", [1, 128], F32),
        ])
        ctx3 = ExitStack()
        sb.update(sbuf_all(ctx3, nc, [
            ("qkvS", [128, 64, 256], BF16),
            ("rawf", [128, 259], F32),
            ("convacc", [128, 256], F32),
            ("tails", [128, 64, 3], F32),
            ("convS", [128, 64, 4], F32),
            ("wqkv0", [128, EC, 128], BF16), ("wqkv1", [128, EC, 128], BF16),
            ("wbas", [128, EC, 64], BF16),
            ("zerob", [128, 128], BF16),
            ("bdlow", [128, 128], I32), ("bdupi", [128, 128], I32),
            ("alog", [64, 1], F32), ("dtb", [64, 1], F32),
            ("negea", [64, 1], F32),
            ("baT", [64, S], F32),
            ("bg", [64, S], F32),
            ("wk1", [64, S], F32),
            ("bgC", [64, 32, 64], F32),
            ("egC", [64, 32, 32], F32),
            ("rkraw", [64, 16], F32), ("rqraw", [64, 16], F32),
            ("rkx", [64, 32], F32), ("rqx", [64, 32], F32),
            ("vecD", [128, 8, 32], F32),
            ("junk", [64, 128], BF16),
            ("glr", [1, 32], F32),
            ("eglb", [128, 32], F32),
            ("ones64", [128, 64], F32), ("negbig", [128, 64], F32),
            ("Atl", [128, 64], F32), ("dgt", [128, 64], F32), ("dgm", [128, 64], F32),
            ("decay", [128, 64], BF16), ("decayT", [128, 64], BF16),
            ("bdlowi", [128, 128], I32),
            ("m64low", [128, 64], I32), ("m64up", [128, 64], I32),
            ("identf2", [128, 64], F32),
            ("vts", [128, 128], BF16),
            ("convt", [128, 256], F32),
            ("stb", [128, NVH, 128], F32),
            ("stbb", [128, NVH, 128], BF16),
            ("kT64", [128, NKH, 128], BF16),
            ("vP", [128, NKH, 128], BF16),
            ("NwM", [128, 128], BF16), ("Nw2", [128, 128], BF16),
            ("SqTm", [128, 128], BF16), ("SqT2", [128, 128], BF16),
            ("Y1", [128, 128], BF16),
            ("P8", [128, 10, 128], BF16),
            ("SAt", [128, 128], BF16), ("SBs", [128, 128], BF16), ("SCs", [128, 128], BF16),
            ("sum1", [128, 128], BF16), ("sum2", [128, 128], BF16),
            ("W1s", [128, 128], BF16), ("TwT2", [128, 128], BF16),
            ("vbP", [128, 128], BF16), ("aP", [128, 128], BF16),
            ("vtl", [128, 128], BF16),
            ("ocoreS", [128, 16, 128], BF16),
        ]))
        pN = {}
        for nm in ("A", "B", "C", "D", "U"):
            pN[nm] = ctx.enter_context(nc.psum_tensor("psb" + nm, [128, 512], F32))
        pT1 = ctx.enter_context(nc.psum_tensor("psT1", [128, 1024], BF16))
        pT2 = ctx.enter_context(nc.psum_tensor("psT2", [128, 1024], BF16))
        pT3 = ctx.enter_context(nc.psum_tensor("psT3", [128, 1024], BF16))
        block = ctx.enter_context(nc.Block())
        em = Em(nc, block, ctx)
        b = dict(sb)

        tk_id = em.dma("sync", sb["identb"][:], dr["ident"])
        tk_idf = em.dma("sync", sb["identf"][:], dr["identf"])
        tk_on = em.dma("sync", sb["onesf"][:], dr["onesf"])
        tk_bl = em.dma("sync", sb["bdlow"][:], dr["bdlow"])
        tk_bu = em.dma("sync", sb["bdupi"][:], dr["bdupi"])
        tk_bli = em.dma("sync", sb["bdlowi"][:], dr["bdlowi"])
        tk_m64l = em.dma("sync", sb["m64low"][:], dr["m64low"])
        tk_m64u = em.dma("sync", sb["m64up"][:], dr["m64up"])
        tk_if2 = em.dma("sync", sb["identf2"][:], dr["identf2"])
        tk_al = em.dma("sync", sb["alog"][32:64, :], dr["alog"])
        tk_dt = em.dma("sync", sb["dtb"][32:64, :], dr["dtb"])
        tk_cw = em.dma("sync", sb["convS"][:], dr["convw"].rearrange("(c p) k -> p c k", p=128))
        tk_ba = em.dma("gpsimd", sb["wbas"][:], dr["wba"].rearrange("(c p) n -> p c n", p=128))
        tk_x = em.dma("sync", sb["xlnT"][:], dr["xln3T"])
        z0 = em.op("vector", lambda V: V.memset(sb["zerob"][:], 0.0))
        z1 = em.op("vector", lambda V: V.memset(sb["tails"][:], 0.0))
        z2 = em.op("vector", lambda V: V.memset(sb["stb"][:], 0.0))
        z3 = em.op("vector", lambda V: V.memset(sb["stbb"][:], 0.0))
        z4 = em.op("vector", lambda V: V.memset(sb["ones64"][:], 1.0))
        z5 = em.op("vector", lambda V: V.memset(sb["negbig"][:], -1e30))
        z6 = em.op("vector", lambda V: V.memset(sb["vecD"][:], 0.0))
        z7 = em.op("vector", lambda V: V.memset(sb["Nw2"][:], 0.0))
        z8 = em.op("vector", lambda V: V.memset(sb["SqT2"][:], 0.0))

        ba_mm = []
        for j in range(4):
            mt = None
            for c in range(EC):
                dep0 = [tk_ba, tk_x] if c == 0 else []
                mt = em.op("tensor", lambda T, j=j, c=c: T.matmul(
                    pN["A"][0:64, :], sb["wbas"][:, c, :],
                    sb["xlnT"][:, c, j * 512:(j + 1) * 512],
                    start=(c == 0), stop=(c == EC - 1)), deps=dep0, inc=(c == EC - 1))
            ev = em.op("vector", lambda V, j=j: V.tensor_copy(
                out=sb["baT"][:, j * 512:(j + 1) * 512], in_=pN["A"][0:64, :]),
                deps=[mt, ba_mm[-1] if ba_mm else None])
            ba_mm.append(ev)
            em.wait("tensor", [ev])
        sgm = em.op("scalar", lambda A: A.activation(
            out=sb["bg"][0:32, :], in_=sb["baT"][0:32, :], func=AF.Sigmoid),
            deps=[ba_mm[-1]])
        t1 = em.op("vector", lambda V: V.tensor_scalar(
            out=sb["baT"][32:64, :], in0=sb["baT"][32:64, :],
            scalar1=sb["dtb"][32:64, :], scalar2=None, op0=ALU.add),
            deps=[ba_mm[-1], tk_dt])
        t2a = em.op("vector", lambda V: V.tensor_scalar(
            out=sb["wk1"][32:64, :], in0=sb["baT"][32:64, :], scalar1=-1.0,
            scalar2=None, op0=ALU.mult), deps=[t1])
        t2 = em.op("vector", lambda V: V.tensor_tensor(
            out=sb["wk1"][32:64, :], in0=sb["baT"][32:64, :], in1=sb["wk1"][32:64, :],
            op=ALU.max), deps=[t2a])
        t3 = em.op("scalar", lambda A: A.activation(
            out=sb["wk1"][32:64, :], in_=sb["wk1"][32:64, :], func=AF.Exp, scale=-1.0),
            deps=[t2])
        t4 = em.op("vector", lambda V: V.tensor_scalar(
            out=sb["wk1"][32:64, :], in0=sb["wk1"][32:64, :], scalar1=1.0,
            scalar2=None, op0=ALU.add), deps=[t3])
        t5 = em.op("scalar", lambda A: A.activation(
            out=sb["wk1"][32:64, :], in_=sb["wk1"][32:64, :], func=AF.Ln), deps=[t4])
        t6 = em.op("vector", lambda V: V.tensor_scalar(
            out=sb["bg"][32:64, :], in0=sb["baT"][32:64, :], scalar1=0.0,
            scalar2=None, op0=ALU.max), deps=[t5, t1, sgm])
        t7 = em.op("vector", lambda V: V.tensor_tensor(
            out=sb["wk1"][32:64, :], in0=sb["wk1"][32:64, :], in1=sb["bg"][32:64, :],
            op=ALU.add), deps=[t6, t5])
        aeg = em.op("scalar", lambda A: A.activation(
            out=sb["negea"][32:64, :], in_=sb["alog"][32:64, :], func=AF.Exp),
            deps=[tk_al])
        aneg = em.op("vector", lambda V: V.tensor_scalar(
            out=sb["negea"][32:64, :], in0=sb["negea"][32:64, :], scalar1=-1.0,
            scalar2=None, op0=ALU.mult), deps=[aeg])
        t8 = em.op("vector", lambda V: V.tensor_scalar(
            out=sb["wk1"][32:64, :], in0=sb["wk1"][32:64, :],
            scalar1=sb["negea"][32:64, :], scalar2=None, op0=ALU.mult),
            deps=[t7, aneg])
        cums = []
        for n in range(32):
            cums.append(em.op("vector", lambda V, n=n: V.tensor_tensor_scan(
                out=sb["bg"][32:64, n * 64:(n + 1) * 64],
                data0=sb["wk1"][32:64, n * 64:(n + 1) * 64],
                data1=sb["wk1"][32:64, n * 64:(n + 1) * 64],
                initial=0.0, op0=ALU.add, op1=ALU.bypass), deps=[t8, t6]))
        bgC_tok = []
        for n in range(32):
            pp = pN["C"]
            tt1 = em.op("tensor", lambda T, n=n, pp=pp: T.transpose(
                pp[0:64, 0:64], sb["bg"][:, n * 64:(n + 1) * 64],
                sb["identf"][0:64, 0:64]),
                deps=[cums[n], sgm, tk_idf, bgC_tok[-1] if bgC_tok else None])
            tt2 = em.op("vector", lambda V, n=n, pp=pp: V.tensor_copy(
                out=sb["bgC"][:, n, :], in_=pp[0:64, 0:64]), deps=[tt1])
            bgC_tok.append(tt2)
        eg1 = em.op("scalar", lambda A: A.activation(
            out=sb["egC"][:], in_=sb["bgC"][:, :, 32:64], func=AF.Exp),
            deps=bgC_tok[-1:])
        em.barrier()

        wq_tok = [None, None]
        wq_free = [None, None]
        wqb = [sb["wqkv0"], sb["wqkv1"]]

        def load_wqkv(ot):
            i = ot % 2
            wq_tok[i] = em.dma("gpsimd", wqb[i][:],
                               dr["wqkv"][ot].rearrange("(c p) o -> p c o", p=128),
                               deps=[wq_free[i]])

        load_wqkv(0)

        qkv_rd_last = None
        for sblk in range(8):
            s0 = sblk * 256
            conv_done = []
            for ot in range(64):
                i = ot % 2
                mt = None
                pq = pN["A"] if ot % 2 == 0 else pN["B"]
                for c in range(EC):
                    dep0 = [wq_tok[i], tk_x, b.get(("qkv_ps", ot % 2))] if c == 0 else []
                    mt = em.op("tensor", lambda T, c=c, i=i, s0=s0, pq=pq: T.matmul(
                        pq[:, 0:256], wqb[i][:, c, :], sb["xlnT"][:, c, s0:s0 + 256],
                        start=(c == 0), stop=(c == EC - 1)), deps=dep0, inc=(c == EC - 1))
                wq_free[i] = mt
                if ot + 1 < 64:
                    load_wqkv(ot + 1)
                elif sblk + 1 < 8:
                    load_wqkv(0)
                c0 = em.op("vector", lambda V, ot=ot: V.tensor_copy(
                    out=sb["rawf"][:, 0:3], in_=sb["tails"][:, ot, :]),
                    deps=[b.get("rawf_free"), b.get(("tails_w", ot)), z1])
                c1 = em.op("vector", lambda V, pq=pq: V.tensor_copy(
                    out=sb["rawf"][:, 3:259], in_=pq[:, 0:256]), deps=[mt, c0])
                b[("qkv_ps", ot % 2)] = c1
                c2 = em.op("vector", lambda V, ot=ot: V.tensor_copy(
                    out=sb["tails"][:, ot, :], in_=sb["rawf"][:, 256:259]), deps=[c1])
                b[("tails_w", ot)] = c2
                eng = "vector" if ot % 2 == 0 else "gpsimd"
                d1 = em.op(eng, lambda X, ot=ot: X.tensor_scalar(
                    out=sb["convacc"][:], in0=sb["rawf"][:, 0:256],
                    scalar1=sb["convS"][:, ot, 0:1], scalar2=None, op0=ALU.mult),
                    deps=[c1, tk_cw, b.get("convacc_free")])
                for j in (1, 2):
                    if eng == "vector":
                        d1 = em.op(eng, lambda X, ot=ot, j=j: X.scalar_tensor_tensor(
                            out=sb["convacc"][:], in0=sb["rawf"][:, j:256 + j],
                            scalar=sb["convS"][:, ot, j:j + 1], in1=sb["convacc"][:],
                            op0=ALU.mult, op1=ALU.add), deps=[d1])
                    else:
                        dt_ = em.op(eng, lambda X, ot=ot, j=j: X.tensor_scalar(
                            out=sb["convt"][:], in0=sb["rawf"][:, j:256 + j],
                            scalar1=sb["convS"][:, ot, j:j + 1], scalar2=None,
                            op0=ALU.mult), deps=[d1, b.get("convt_free")])
                        d1 = em.op(eng, lambda X: X.tensor_tensor(
                            out=sb["convacc"][:], in0=sb["convacc"][:],
                            in1=sb["convt"][:], op=ALU.add), deps=[dt_])
                        b["convt_free"] = d1
                d1 = em.op("vector", lambda X, ot=ot: X.scalar_tensor_tensor(
                    out=sb["convacc"][:], in0=sb["rawf"][:, 3:259],
                    scalar=sb["convS"][:, ot, 3:4], in1=sb["convacc"][:],
                    op0=ALU.mult, op1=ALU.add), deps=[d1])
                b["rawf_free"] = d1
                sl = em.op("scalar", lambda A, ot=ot: A.activation(
                    out=sb["qkvS"][:, ot, :], in_=sb["convacc"][:], func=AF.Silu),
                    deps=[d1, qkv_rd_last, b.get(("qkvS_w", ot))])
                b[("qkvS_w", ot)] = sl
                b["convacc_free"] = sl
                conv_done.append(sl)

            for n8 in range(4):
                n = sblk * 4 + n8
                cs = n8 * 64
                kt_toks = {}
                vp_toks = {}
                sq_last = None
                for p in range(NKH):
                    ptile = pT1 if p % 2 == 0 else pT2
                    ktt0 = em.op("tensor", lambda T, p=p, cs=cs, ptile=ptile: T.transpose(
                        ptile[0:64, 0:128], sb["qkvS"][:, 16 + p, cs:cs + 64],
                        sb["identb"][:]),
                        deps=[conv_done[16 + p], tk_id, b.get(("pT_free", p % 2))],
                        inc=False)
                    ktt = em.op("tensor", lambda T, p=p, cs=cs, ptile=ptile: T.transpose(
                        ptile[64:128, 0:128], sb["qkvS"][:, 16 + p, cs:cs + 64],
                        sb["identb"][:], tile_position=(0, 64)), deps=[])
                    qtt = em.op("tensor", lambda T, p=p, cs=cs, ptile=ptile: T.transpose(
                        ptile[0:64, 128:256], sb["qkvS"][:, p, cs:cs + 64],
                        sb["identb"][:]),
                        deps=[conv_done[p]])
                    ke = em.op("vector", lambda V, p=p, ptile=ptile: V.tensor_copy(
                        out=sb["kT64"][:, p, :], in_=ptile[:, 0:128]),
                        deps=[ktt, qtt, b.get(("kT64_r", p))])
                    ks = em.op("scalar", lambda A, p=p, ptile=ptile: A.activation(
                        out=sb["junk"][:], in_=ptile[0:64, 0:128], func=AF.Square,
                        accum_out=sb["rkraw"][:, p:p + 1]),
                        deps=[ktt, qtt, b.get("rk_rd"), b.get("junk_w")])
                    qs = em.op("scalar", lambda A, p=p, ptile=ptile: A.activation(
                        out=sb["junk"][:], in_=ptile[0:64, 128:256], func=AF.Square,
                        accum_out=sb["rqraw"][:, p:p + 1]), deps=[qtt, ks])
                    b["junk_w"] = qs
                    sq_last = qs
                    kt_toks[p] = ke
                    b[("pT_free", p % 2)] = [qs, ke]
                    vtile = pT3
                    v0 = em.op("tensor", lambda T, p=p, cs=cs, vtile=vtile: T.transpose(
                        vtile[0:64, 0:128], sb["qkvS"][:, 32 + 2 * p, cs:cs + 64],
                        sb["identb"][:]),
                        deps=[conv_done[32 + 2 * p], b.get("pT3_free"),
                              b.get("pT3b_rd")])
                    v1 = em.op("tensor", lambda T, p=p, cs=cs, vtile=vtile: T.transpose(
                        vtile[64:128, 0:128], sb["qkvS"][:, 33 + 2 * p, cs:cs + 64],
                        sb["identb"][:], tile_position=(0, 64)),
                        deps=[conv_done[33 + 2 * p]])
                    vv = em.op("vector", lambda V, p=p, vtile=vtile: V.tensor_copy(
                        out=sb["vP"][:, p, :], in_=vtile[:, 0:128]),
                        deps=[v0, v1, b.get(("vP_r", p))])
                    b["pT3_free"] = vv
                    vp_toks[p] = vv
                dl = [sq_last, eg1, b.get("vec_rd")]
                r1 = em.op("vector", lambda V: V.tensor_scalar(
                    out=sb["rkraw"][:], in0=sb["rkraw"][:], scalar1=1e-6, scalar2=None,
                    op0=ALU.add), deps=dl)
                r2 = em.op("scalar", lambda A: A.activation(
                    out=sb["rkraw"][:], in_=sb["rkraw"][:], func=AF.Ln), deps=[r1])
                r3 = em.op("scalar", lambda A: A.activation(
                    out=sb["rkraw"][:], in_=sb["rkraw"][:], func=AF.Exp, scale=-0.5),
                    deps=[r2])
                r4 = em.op("vector", lambda V: V.tensor_scalar(
                    out=sb["rqraw"][:], in0=sb["rqraw"][:], scalar1=1e-6, scalar2=None,
                    op0=ALU.add), deps=dl)
                r5 = em.op("scalar", lambda A: A.activation(
                    out=sb["rqraw"][:], in_=sb["rqraw"][:], func=AF.Ln), deps=[r4])
                r6 = em.op("scalar", lambda A: A.activation(
                    out=sb["rqraw"][:], in_=sb["rqraw"][:], func=AF.Exp, scale=-0.5),
                    deps=[r5])
                x1 = em.op("vector", lambda V: V.tensor_copy(
                    out=sb["rkx"][:, 0::2], in_=sb["rkraw"][:]), deps=[r3, b.get("rkx_rd")])
                x2 = em.op("vector", lambda V: V.tensor_copy(
                    out=sb["rkx"][:, 1::2], in_=sb["rkraw"][:]), deps=[r3, x1])
                x3 = em.op("vector", lambda V: V.tensor_copy(
                    out=sb["rqx"][:, 0::2], in_=sb["rqraw"][:]), deps=[r6, b.get("rqx_rd")])
                x4 = em.op("vector", lambda V: V.tensor_copy(
                    out=sb["rqx"][:, 1::2], in_=sb["rqraw"][:]), deps=[r6, x3])
                vD = sb["vecD"]
                vdw = b.get("vecD_rd")
                y1 = em.op("vector", lambda V, n=n: V.tensor_tensor(
                    out=vD[0:64, 2, :], in0=sb["rkx"][:], in1=sb["bgC"][:, n, 0:32],
                    op=ALU.mult), deps=[x2, bgC_tok[-1], vdw])
                y2 = em.op("vector", lambda V: V.tensor_tensor(
                    out=vD[0:64, 0, :], in0=vD[0:64, 2, :], in1=sb["rkx"][:],
                    op=ALU.mult), deps=[y1, vdw])
                y3 = em.op("vector", lambda V: V.tensor_scalar(
                    out=vD[0:64, 0, :], in0=vD[0:64, 0, :], scalar1=-1.0, scalar2=None,
                    op0=ALU.mult), deps=[y2])
                y4 = em.op("vector", lambda V, n=n: V.tensor_tensor(
                    out=vD[0:64, 1, :], in0=vD[0:64, 0, :], in1=sb["egC"][:, n, :],
                    op=ALU.mult), deps=[y3, vdw])
                y5 = em.op("vector", lambda V, n=n: V.tensor_tensor(
                    out=vD[0:64, 3, :], in0=sb["rqx"][:], in1=sb["egC"][:, n, :],
                    op=ALU.mult), deps=[x4, vdw])
                y6 = em.op("vector", lambda V: V.tensor_scalar(
                    out=vD[0:64, 3, :], in0=vD[0:64, 3, :], scalar1=DK ** -0.5,
                    scalar2=None, op0=ALU.mult), deps=[y5])
                y7 = em.op("vector", lambda V: V.tensor_scalar(
                    out=vD[0:64, 4, :], in0=sb["rqx"][:], scalar1=DK ** -0.5,
                    scalar2=None, op0=ALU.mult), deps=[x4, vdw])
                yg = em.op("vector", lambda V, n=n: V.tensor_copy(
                    out=vD[0:64, 6, :], in_=sb["bgC"][:, n, 32:64]),
                    deps=[bgC_tok[-1], vdw])
                ge1 = em.dma("gpsimd", sb["glr"][:], sb["bgC"][63:64, n, 32:64],
                             deps=[bgC_tok[-1], b.get("glr_rd")])
                ge3 = em.op("tensor", lambda T: T.matmul(
                    pN["C"][:, 0:32], sb["onesf"][:], sb["glr"][:],
                    start=True, stop=True),
                    deps=[ge1, tk_on, b.get("psC_rd"),
                          b.get(("psCst_rd", 0)), b.get(("psCst_rd", 1))])
                b["glr_rd"] = ge3
                ge4 = em.op("vector", lambda V, n=n: V.tensor_tensor(
                    out=vD[0:64, 5, :], in0=pN["C"][0:64, 0:32],
                    in1=sb["bgC"][:, n, 32:64], op=ALU.subtract),
                    deps=[ge3, vdw])
                ge5 = em.op("scalar", lambda A: A.activation(
                    out=vD[0:64, 5, :], in_=vD[0:64, 5, :], func=AF.Exp), deps=[ge4])
                ge6 = em.op("vector", lambda V: V.tensor_copy(
                    out=sb["eglb"][:], in_=pN["C"][:, 0:32]),
                    deps=[ge3, b.get("eglb_rd")])
                ge7 = em.op("scalar", lambda A: A.activation(
                    out=sb["eglb"][:], in_=sb["eglb"][:], func=AF.Exp), deps=[ge6])
                b["psC_rd"] = ge6
                tdup = em.dma("gpsimd", vD[64:128, 0:7, :], vD[0:64, 0:7, :],
                              deps=[y3, y4, y1, y6, y7, yg, ge5, vdw, z6])
                vecs = {"wneg": y3, "wE": y4, "brk": y1, "rqE": y6, "rqs": y7,
                        "dup": tdup}
                for p in range(NKH):
                    vh0, vh1 = 2 * p, 2 * p + 1
                    kna = sb["qkvS"][:, 16 + p, cs:cs + 64]
                    qna = sb["qkvS"][:, p, cs:cs + 64]
                    da1 = em.op("vector", lambda V, vh0=vh0: V.tensor_scalar(
                        out=sb["Atl"][0:64, :], in0=sb["ones64"][0:64, :],
                        scalar1=sb["vecD"][0:64, 6, vh0:vh0 + 1], scalar2=None,
                        op0=ALU.mult), deps=[yg, z4, b.get("Atl_rd")])
                    da2 = em.op("vector", lambda V, vh1=vh1: V.tensor_scalar(
                        out=sb["Atl"][64:128, :], in0=sb["ones64"][64:128, :],
                        scalar1=sb["vecD"][64:128, 6, vh1:vh1 + 1], scalar2=None,
                        op0=ALU.mult), deps=[vecs["dup"], z4])
                    dat = em.op("tensor", lambda T: T.matmul(
                        pN["U"][0:64, 128:192], sb["Atl"][0:64, :],
                        sb["identf2"][0:64, :], start=True, stop=True),
                        deps=[da1, tk_if2, b.get("psU_rd")], inc=False)
                    dat2 = em.op("tensor", lambda T: T.matmul(
                        pN["U"][64:128, 128:192], sb["Atl"][64:128, :],
                        sb["identf2"][64:128, :], start=True, stop=True,
                        tile_position=(64, 64)), deps=[da2], inc=True)
                    dg1 = em.op("vector", lambda V: V.tensor_tensor(
                        out=sb["dgt"][:], in0=sb["Atl"][:], in1=pN["U"][:, 128:192],
                        op=ALU.subtract), deps=[dat2, da1, da2, b.get("dgt_rd")])
                    b["psU_rd"] = dg1
                    dg2 = em.op("vector", lambda V: V.select(
                        out=sb["dgm"][:], mask=sb["m64low"][:], on_true=sb["dgt"][:],
                        on_false=sb["negbig"][:], add_drain=True), deps=[dg1, tk_m64l, z5, b.get("dgm_rd")])
                    dg3 = em.op("scalar", lambda A: A.activation(
                        out=sb["decay"][:], in_=sb["dgm"][:], func=AF.Exp),
                        deps=[dg2, b.get("decay_rd")])
                    dg4 = em.op("vector", lambda V: V.tensor_scalar(
                        out=sb["dgt"][:], in0=sb["dgt"][:], scalar1=-1.0, scalar2=None,
                        op0=ALU.mult), deps=[dg2])
                    b["dgt_rd"] = dg4
                    dg5 = em.op("vector", lambda V: V.select(
                        out=sb["dgm"][:], mask=sb["m64up"][:], on_true=sb["dgt"][:],
                        on_false=sb["negbig"][:], add_drain=True), deps=[dg4, tk_m64u, z5, dg3])
                    b["dgm_rd"] = dg5
                    dg6 = em.op("scalar", lambda A: A.activation(
                        out=sb["decayT"][:], in_=sb["dgm"][:], func=AF.Exp),
                        deps=[dg5, b.get("decayT_rd")])
                    m1 = em.op("tensor", lambda T, kna=kna: T.matmul(
                        pN["A"][0:64, 0:64], kna, kna, start=True, stop=True),
                        deps=[conv_done[16 + p], b.get("psA_rd")], inc=False)
                    m2 = em.op("tensor", lambda T, kna=kna: T.matmul(
                        pN["A"][64:128, 0:64], kna, kna, start=True, stop=True,
                        tile_position=(0, 64)), inc=False)
                    m3 = em.op("tensor", lambda T, kna=kna, qna=qna: T.matmul(
                        pN["A"][0:64, 64:128], kna, qna, start=True, stop=True), inc=False)
                    m4 = em.op("tensor", lambda T, kna=kna, qna=qna: T.matmul(
                        pN["A"][64:128, 64:128], kna, qna, start=True, stop=True,
                        tile_position=(0, 64)), inc=True)
                    e1 = em.op("vector", lambda V, vh0=vh0: V.tensor_scalar(
                        out=sb["Nw2"][0:64, 0:64], in0=pN["A"][0:64, 0:64],
                        scalar1=sb["vecD"][0:64, 0, vh0:vh0 + 1], scalar2=None,
                        op0=ALU.mult),
                        deps=[m4, vecs["wneg"], b.get("Nw2_rd"), z7])
                    e2 = em.op("vector", lambda V, vh1=vh1: V.tensor_scalar(
                        out=sb["Nw2"][64:128, 64:128], in0=pN["A"][64:128, 0:64],
                        scalar1=sb["vecD"][64:128, 0, vh1:vh1 + 1], scalar2=None,
                        op0=ALU.mult),
                        deps=[m4, vecs["dup"]])
                    e2b1 = em.op("vector", lambda V: V.tensor_tensor(
                        out=sb["Nw2"][0:64, 0:64], in0=sb["Nw2"][0:64, 0:64],
                        in1=sb["decay"][0:64, :], op=ALU.mult), deps=[e1, dg3])
                    e2b = em.op("vector", lambda V: V.tensor_tensor(
                        out=sb["Nw2"][64:128, 64:128], in0=sb["Nw2"][64:128, 64:128],
                        in1=sb["decay"][64:128, :], op=ALU.mult), deps=[e2, dg3])
                    b["decay_rd"] = e2b
                    e3 = em.op("vector", lambda V: V.select(
                        out=sb["NwM"][:], mask=sb["bdlow"][:], on_true=sb["Nw2"][:],
                        on_false=sb["zerob"][:], add_drain=True),
                        deps=[e2b1, e2b, tk_bl, z0, b.get("NwM_rd")])
                    b["Nw2_rd"] = e3
                    e4 = em.op("scalar", lambda A: A.copy(
                        out=sb["SqT2"][0:64, 0:64], in_=pN["A"][0:64, 64:128]),
                        deps=[m4, b.get("SqT2_rd"), z8])
                    e5 = em.op("scalar", lambda A: A.copy(
                        out=sb["SqT2"][64:128, 64:128], in_=pN["A"][64:128, 64:128]),
                        deps=[m4])
                    e5b1 = em.op("vector", lambda V: V.tensor_tensor(
                        out=sb["SqT2"][0:64, 0:64], in0=sb["SqT2"][0:64, 0:64],
                        in1=sb["decayT"][0:64, :], op=ALU.mult), deps=[e4, dg6])
                    e5b = em.op("vector", lambda V: V.tensor_tensor(
                        out=sb["SqT2"][64:128, 64:128], in0=sb["SqT2"][64:128, 64:128],
                        in1=sb["decayT"][64:128, :], op=ALU.mult), deps=[e5, dg6])
                    b["decayT_rd"] = e5b
                    e6 = em.op("vector", lambda V: V.select(
                        out=sb["SqTm"][:], mask=sb["bdupi"][:], on_true=sb["SqT2"][:],
                        on_false=sb["zerob"][:], add_drain=True),
                        deps=[e5b1, e5b, tk_bu, z0, b.get("SqTm_rd")])
                    b["SqT2_rd"] = e6
                    b["psA_rd"] = (e6[0], max(e6[1], e3[1]))
                    yt = em.op("tensor", lambda T: T.transpose(
                        pT3[:, 128:256], sb["NwM"][:], sb["identb"][:]),
                        deps=[e3, b.get("pT3b_rd"), vp_toks[NKH - 1]])
                    ye = em.op("scalar", lambda A: A.copy(
                        out=sb["Y1"][:], in_=pT3[:, 128:256]), deps=[yt, b.get("Y1_rd")])
                    b["pT3b_rd"] = ye
                    P8 = sb["P8"]
                    NwMs = sb["NwM"][:]
                    Y1s = sb["Y1"][:]
                    def mm(out_ps, lt, rh, dep, tp=None, inc=True):
                        if tp is None:
                            return em.op("tensor", lambda T, out_ps=out_ps, lt=lt, rh=rh:
                                         T.matmul(out_ps, lt, rh, start=True, stop=True),
                                         deps=dep, inc=inc)
                        return em.op("tensor", lambda T, out_ps=out_ps, lt=lt, rh=rh, tp=tp:
                                     T.matmul(out_ps, lt, rh, start=True, stop=True,
                                              tile_position=tp), deps=dep, inc=inc)
                    p1 = mm(pN["D"][:, 0:128], Y1s, NwMs, [ye, e3, b.get("psD_rd")],
                            inc=False)
                    p2 = mm(pN["D"][:, 128:256], NwMs, Y1s, [], inc=True)
                    ev1 = em.op("vector", lambda V: V.tensor_copy(
                        out=P8[:, 0:2, :], in_=pN["D"][:, 0:256]),
                        deps=[p2, b.get("P8_rd")])
                    p3 = mm(pN["U"][:, 0:128], P8[:, 0, :], Y1s, [ev1, b.get("psU_rd")],
                            inc=False)
                    p4 = mm(pN["U"][:, 128:256], P8[:, 1, :], P8[:, 0, :], [], inc=False)
                    p5 = mm(pN["U"][:, 256:384], P8[:, 0, :], P8[:, 1, :], [], inc=True)
                    ev2 = em.op("vector", lambda V: V.tensor_copy(
                        out=P8[:, 2:5, :], in_=pN["U"][:, 0:384]), deps=[p5])
                    p6 = mm(pN["D"][:, 0:128], P8[:, 4, :], P8[:, 3, :], [ev2, ev1],
                            inc=False)
                    p7 = mm(pN["D"][:, 128:256], P8[:, 3, :], P8[:, 4, :], [], inc=True)
                    ev3 = em.op("vector", lambda V: V.tensor_copy(
                        out=P8[:, 5:7, :], in_=pN["D"][:, 0:256]), deps=[p7])
                    p8_ = mm(pN["U"][:, 0:128], P8[:, 6, :], P8[:, 5, :], [ev3], inc=False)
                    p9 = mm(pN["U"][:, 128:256], P8[:, 5, :], P8[:, 6, :], [], inc=False)
                    p10 = mm(pN["U"][:, 256:384], P8[:, 4, :], P8[:, 5, :], [], inc=True)
                    ev4 = em.op("vector", lambda V: V.tensor_copy(
                        out=P8[:, 7:9, :], in_=pN["U"][:, 0:256]), deps=[p10])
                    p11 = mm(pN["D"][:, 0:128], P8[:, 8, :], P8[:, 7, :], [ev4], inc=True)
                    ev5 = em.op("vector", lambda V: V.tensor_copy(
                        out=P8[:, 9:10, :], in_=pN["D"][:, 0:128]), deps=[p11])
                    p12 = mm(pN["D"][:, 128:256], P8[:, 8, :], P8[:, 9, :], [ev5], inc=True)
                    s1 = em.op("vector", lambda V: V.tensor_tensor(
                        out=sb["sum1"][:], in0=Y1s, in1=P8[:, 1, :], op=ALU.add),
                        deps=[ev2, b.get("sum1_rd")])
                    s2 = em.op("vector", lambda V: V.tensor_tensor(
                        out=sb["sum2"][:], in0=pN["U"][:, 0:128], in1=sb["identb"][:],
                        op=ALU.add), deps=[ev2, tk_id, b.get("sum2_rd")])
                    s3 = em.op("vector", lambda V: V.tensor_tensor(
                        out=sb["SAt"][:], in0=sb["sum1"][:], in1=sb["sum2"][:],
                        op=ALU.add), deps=[s1, s2, b.get("SAt_rd")])
                    s4 = em.op("vector", lambda V: V.tensor_tensor(
                        out=sb["sum1"][:], in0=P8[:, 3, :], in1=P8[:, 5, :], op=ALU.add),
                        deps=[ev3, s3])
                    s5 = em.op("vector", lambda V: V.tensor_tensor(
                        out=sb["sum2"][:], in0=pN["U"][:, 256:384], in1=sb["identb"][:],
                        op=ALU.add), deps=[p10, s3])
                    s6 = em.op("vector", lambda V: V.tensor_tensor(
                        out=sb["SBs"][:], in0=sb["sum1"][:], in1=sb["sum2"][:],
                        op=ALU.add), deps=[s4, s5, b.get("SBs_rd")])
                    b["psU_rd"] = s6
                    s7 = em.op("vector", lambda V: V.tensor_tensor(
                        out=sb["sum1"][:], in0=P8[:, 7, :], in1=P8[:, 9, :], op=ALU.add),
                        deps=[ev5, s6])
                    s8 = em.op("vector", lambda V: V.tensor_tensor(
                        out=sb["sum2"][:], in0=pN["D"][:, 128:256], in1=sb["identb"][:],
                        op=ALU.add), deps=[p12, s6])
                    s9 = em.op("vector", lambda V: V.tensor_tensor(
                        out=sb["SCs"][:], in0=sb["sum1"][:], in1=sb["sum2"][:],
                        op=ALU.add), deps=[s7, s8, b.get("SCs_rd")])
                    b["sum1_rd"] = s9
                    b["sum2_rd"] = s9
                    b["P8_rd"] = s9
                    b["Y1_rd"] = s9
                    pw = mm(pN["D"][:, 256:384], sb["SBs"][:], sb["SAt"][:],
                            [s6, s3, s9], inc=True)
                    we = em.op("vector", lambda V: V.tensor_copy(
                        out=sb["W1s"][:], in_=pN["D"][:, 256:384]),
                        deps=[pw, b.get("W1s_rd")])
                    b["SAt_rd"] = we
                    b["SBs_rd"] = we
                    pt_ = mm(pN["D"][:, 384:512], sb["SCs"][:], sb["W1s"][:],
                             [we, s9], inc=True)
                    te = em.op("vector", lambda V: V.tensor_copy(
                        out=sb["TwT2"][:], in_=pN["D"][:, 384:512]),
                        deps=[pt_, b.get("TwT2_rd")])
                    b["SCs_rd"] = te
                    b["psD_rd"] = te
                    b["W1s_rd"] = pt_
                    b["NwM_rd"] = p2
                    k1 = em.op("tensor", lambda T, kna=kna, vh0=vh0: T.matmul(
                        pN["B"][0:64, 0:128], kna, sb["stbb"][:, vh0, :],
                        start=True, stop=True),
                        deps=[conv_done[16 + p], b.get(("stbb_w", vh0)),
                              b.get("psB_rd"), b.get("psB2_rd"), b.get("psB3_rd"),
                              b.get("psB4_rd")],
                        inc=False)
                    k2 = em.op("tensor", lambda T, kna=kna, vh1=vh1: T.matmul(
                        pN["B"][64:128, 0:128], kna, sb["stbb"][:, vh1, :],
                        start=True, stop=True, tile_position=(0, 64)),
                        deps=[b.get(("stbb_w", vh1)), z3], inc=True)
                    a1_ = em.op("vector", lambda V, p=p, vh0=vh0: V.tensor_scalar(
                        out=sb["vbP"][0:64, :], in0=sb["vP"][0:64, p, :],
                        scalar1=sb["vecD"][0:64, 2, vh0:vh0 + 1], scalar2=None,
                        op0=ALU.mult),
                        deps=[vp_toks[p], vecs["brk"], b.get("vbP_rd")])
                    a2_ = em.op("vector", lambda V, p=p, vh1=vh1: V.tensor_scalar(
                        out=sb["vbP"][64:128, :], in0=sb["vP"][64:128, p, :],
                        scalar1=sb["vecD"][64:128, 2, vh1:vh1 + 1], scalar2=None,
                        op0=ALU.mult),
                        deps=[vp_toks[p], vecs["dup"]])
                    a3_ = em.op("vector", lambda V, vh0=vh0: V.scalar_tensor_tensor(
                        out=sb["aP"][0:64, :], in0=pN["B"][0:64, 0:128],
                        scalar=sb["vecD"][0:64, 1, vh0:vh0 + 1], in1=sb["vbP"][0:64, :],
                        op0=ALU.mult, op1=ALU.add), deps=[k2, a1_, vecs["wE"],
                                                          b.get("aP_rd")])
                    a4_ = em.op("vector", lambda V, vh1=vh1: V.scalar_tensor_tensor(
                        out=sb["aP"][64:128, :], in0=pN["B"][64:128, 0:128],
                        scalar=sb["vecD"][64:128, 1, vh1:vh1 + 1],
                        in1=sb["vbP"][64:128, :],
                        op0=ALU.mult, op1=ALU.add), deps=[k2, a2_, vecs["dup"]])
                    b["vbP_rd"] = a4_
                    vt1 = em.op("tensor", lambda T: T.matmul(
                        pN["B"][:, 128:256], sb["TwT2"][:], sb["aP"][:],
                        start=True, stop=True),
                        deps=[te, a3_, a4_, b.get("psB2_rd"), b.get("psB4_rd")],
                        inc=True)
                    b["psB_rd"] = (a4_[0], max(a4_[1], a3_[1]))
                    vte = em.op("vector", lambda V: V.tensor_copy(
                        out=sb["vtl"][:], in_=pN["B"][:, 128:256]),
                        deps=[vt1, b.get("vtl_rd")])
                    b["TwT2_rd"] = vte
                    b["psB2_rd"] = vte
                    w1_ = em.op("vector", lambda V, vh0=vh0: V.tensor_scalar(
                        out=sb["vts"][0:64, :], in0=sb["vtl"][0:64, :],
                        scalar1=sb["vecD"][0:64, 5, vh0:vh0 + 1], scalar2=None,
                        op0=ALU.mult),
                        deps=[vte, ge5, b.get("vts_rd")])
                    w2_ = em.op("vector", lambda V, vh1=vh1: V.tensor_scalar(
                        out=sb["vts"][64:128, :], in0=sb["vtl"][64:128, :],
                        scalar1=sb["vecD"][64:128, 5, vh1:vh1 + 1], scalar2=None,
                        op0=ALU.mult),
                        deps=[vte, vecs["dup"]])
                    o1 = em.op("tensor", lambda T, qna=qna, vh0=vh0: T.matmul(
                        pN["B"][0:64, 256:384], qna, sb["stbb"][:, vh0, :],
                        start=True, stop=True),
                        deps=[conv_done[p], b.get("psB3_rd"), b.get("psB4_rd"), vte],
                        inc=False)
                    o2 = em.op("tensor", lambda T, qna=qna, vh1=vh1: T.matmul(
                        pN["B"][64:128, 256:384], qna, sb["stbb"][:, vh1, :],
                        start=True, stop=True, tile_position=(0, 64)), inc=True)
                    o3 = em.op("tensor", lambda T: T.matmul(
                        pN["B"][:, 384:512], sb["SqTm"][:], sb["vtl"][:],
                        start=True, stop=True), deps=[e6, vte, b.get("psB4_rd")],
                        inc=True)
                    b["SqTm_rd"] = o3
                    oi1 = em.op("vector", lambda V, vh0=vh0: V.tensor_scalar(
                        out=sb["vbP"][0:64, :], in0=pN["B"][0:64, 256:384],
                        scalar1=sb["vecD"][0:64, 3, vh0:vh0 + 1], scalar2=None,
                        op0=ALU.mult),
                        deps=[o3, vecs["rqE"], a4_])
                    oi2 = em.op("vector", lambda V, vh1=vh1: V.tensor_scalar(
                        out=sb["vbP"][64:128, :], in0=pN["B"][64:128, 256:384],
                        scalar1=sb["vecD"][64:128, 3, vh1:vh1 + 1], scalar2=None,
                        op0=ALU.mult),
                        deps=[o2, vecs["dup"]])
                    b["psB3_rd"] = oi2
                    oe1 = em.op("vector", lambda V, p=p, vh0=vh0: V.scalar_tensor_tensor(
                        out=sb["ocoreS"][0:64, p, :], in0=pN["B"][0:64, 384:512],
                        scalar=sb["vecD"][0:64, 4, vh0:vh0 + 1], in1=sb["vbP"][0:64, :],
                        op0=ALU.mult, op1=ALU.add),
                        deps=[o3, oi1, vecs["rqs"], b.get(("ocoreS_r", p))])
                    oe2 = em.op("vector", lambda V, p=p, vh1=vh1: V.scalar_tensor_tensor(
                        out=sb["ocoreS"][64:128, p, :], in0=pN["B"][64:128, 384:512],
                        scalar=sb["vecD"][64:128, 4, vh1:vh1 + 1],
                        in1=sb["vbP"][64:128, :],
                        op0=ALU.mult, op1=ALU.add),
                        deps=[o3, oi2, vecs["dup"]])
                    b["vbP_rd"] = oe2
                    b["psB4_rd"] = oe2
                    b["psB_rd"] = oe2
                    b[("ocoreS_w", p)] = (oe2[0], max(oe1[1], oe2[1]))
                    su3 = None
                    for r, vh in ((0, vh0), (1, vh1)):
                        su1 = em.op("tensor", lambda T, p=p, r=r: T.matmul(
                            pN["C"][:, 192 + 128 * r:192 + 128 * (r + 1)],
                            sb["kT64"][64 * r:64 * (r + 1), p, :],
                            sb["vts"][64 * r:64 * (r + 1), :],
                            start=True, stop=True,
                            tile_position=(64 * r, 0)),
                            deps=[kt_toks[p], w1_ if r == 0 else w2_,
                                  b.get(("psCst_rd", r)), ge4, ge6, su3], inc=True)
                        su2 = em.op("vector", lambda V, vh=vh: V.tensor_scalar(
                            out=sb["stb"][:, vh, :], in0=sb["stb"][:, vh, :],
                            scalar1=sb["eglb"][:, vh:vh + 1], scalar2=None, op0=ALU.mult),
                            deps=[ge7, z2, b.get(("stb_w", vh)),
                                  k1 if r == 0 else k2, o1 if r == 0 else o2])
                        su3 = em.op("vector", lambda V, vh=vh, r=r: V.tensor_tensor(
                            out=sb["stb"][:, vh, :], in0=sb["stb"][:, vh, :],
                            in1=pN["C"][:, 192 + 128 * r:192 + 128 * (r + 1)],
                            op=ALU.add), deps=[su1, su2])
                        su4 = em.op("vector", lambda V, vh=vh: V.tensor_copy(
                            out=sb["stbb"][:, vh, :], in_=sb["stb"][:, vh, :]),
                            deps=[su3, z3])
                        b[("stb_w", vh)] = su4
                        b[("stbb_w", vh)] = su4
                        b[("psCst_rd", r)] = su3
                    b[("kT64_r", p)] = su1
                    b[("vP_r", p)] = a2_
                    b["vtl_rd"] = su1
                    b["vts_rd"] = su1
                    b["aP_rd"] = vt1
                    b["Atl_rd"] = dg1
                    qkv_rd_last = o3
                b["eglb_rd"] = su3
                b["rk_rd"] = x4
                b["rkx_rd"] = y2
                b["rqx_rd"] = y7
                b["vecD_rd"] = oe2
                b["vec_rd"] = oe2
                oview = dr["ocore"][n * 64:(n + 1) * 64, :].rearrange(
                    "s (p r d) -> s p r d", p=16, r=2, d=128)
                odeps = [b[("ocoreS_w", pp_)] for pp_ in range(NKH)]
                od1 = em.dma("sync", oview[:, :, 0, :], sb["ocoreS"][0:64], deps=odeps)
                odma = em.dma("sync", oview[:, :, 1, :], sb["ocoreS"][64:128],
                              deps=odeps)
                for pp_ in range(NKH):
                    b[("ocoreS_r", pp_)] = [od1, odma]
        em.barrier()
        ctx3.close()

        sb2 = sbuf_all(ctx, nc, [
            ("normbc", [128, 512], BF16),
            ("xgT", [128, 32, 1024], BF16),
            ("wbuf0", [128, 32, 512], BF16), ("wbuf1", [128, 16, 512], BF16),
            ("ocot", [128, 4096], BF16),
            ("zs", [128, 512], F32),
            ("xgs", [128, 512], F32),
            ("ssq4", [128, 4], F32),
            ("r3t", [128, E], F32),
            ("lnstage", [128, EC, 128], BF16),
            ("ln_s1", [128, 1], F32), ("ln_ssq", [128, 1], F32),
            ("ln_mean", [128, 1], F32), ("ln_msq", [128, 1], F32),
            ("ln_var", [128, 1], F32), ("ln_rstd", [128, 1], F32),
            ("ln_bf", [128, E], BF16),
        ])
        sb2["xr2s"] = sb2["zs"]
        sb2["oslc"] = sb2["xgs"]
        for k, v in sb2.items():
            b[k] = v
        b["tp_ps"] = [pT1, pT2]
        b["tp_free"] = [None, None]
        b["ident_tok"] = tk_id
        b["ident"] = sb["identb"]
        b["ln_prev"] = None
        b["ln_t1last"] = None
        tk_nw = em.dma("gpsimd", sb2["normbc"][:],
                       dr["normw"][:, 0:512].partition_broadcast(128))
        wzb = [sb2["wbuf0"], sb2["wbuf1"]]
        wz_tok = [None, None]
        wz_free = [None, None]
        for hf in range(2):
            s0h = hf * 1024
            xg_done = []
            oco_tok = b.get("oco_tok")
            for zc in range(8):
                zi = zc % 2
                wz_tok[zi] = em.dma("gpsimd", wzb[zi][:, 0:16, :],
                                    dr["wz"].rearrange("(c p) z -> p c z", p=128)
                                    [:, :, zc * 512:(zc + 1) * 512],
                                    deps=[wz_free[zi]])
                for st in range(8):
                    sg = s0h + st * 128
                    if zc == 0 and st == 0:
                        oco_tok = em.dma("sync", sb2["ocot"][:],
                                         dr["ocore"][sg:sg + 128, :],
                                         deps=[b.get("ocot_rd")])
                    pz = pN["A"] if zc % 2 == 0 else pN["B"]
                    zmm = None
                    for c in range(EC):
                        dep0 = [wz_tok[zi], tk_x, b.get(("psZ_rd", zc % 2))] if c == 0 else []
                        zmm = em.op("tensor", lambda T, c=c, sg=sg, zi=zi, pz=pz: T.matmul(
                            pz[:], sb["xlnT"][:, c, sg:sg + 128], wzb[zi][:, c, :],
                            start=(c == 0), stop=(c == EC - 1)), deps=dep0,
                            inc=(c == EC - 1))
                    wz_free[zi] = zmm
                    zsl = em.op("scalar", lambda A, pz=pz: A.activation(
                        out=sb2["zs"][:], in_=pz[:], func=AF.Silu),
                        deps=[zmm, b.get("zs_rd")])
                    b[("psZ_rd", zc % 2)] = zsl
                    xg1 = em.op("vector", lambda V, zc=zc: V.tensor_tensor(
                        out=sb2["xgs"][:], in0=sb2["zs"][:],
                        in1=sb2["ocot"][:, zc * 512:(zc + 1) * 512], op=ALU.mult),
                        deps=[zsl, oco_tok, b.get("xgs_rd")])
                    sqs = []
                    for r in range(4):
                        sqs.append(em.op("scalar", lambda A, r=r: A.activation(
                            out=sb2["zs"][:, r * 128:(r + 1) * 128],
                            in_=sb2["xgs"][:, r * 128:(r + 1) * 128], func=AF.Square,
                            accum_out=sb2["ssq4"][:, r:r + 1]),
                            deps=[xg1, b.get("ssq4_rd")]))
                    b["zs_rd"] = sqs[-1]
                    q1 = em.op("vector", lambda V: V.tensor_scalar(
                        out=sb2["ssq4"][:], in0=sb2["ssq4"][:], scalar1=1.0 / 128,
                        scalar2=1e-6, op0=ALU.mult, op1=ALU.add), deps=sqs)
                    q2 = em.op("scalar", lambda A: A.activation(
                        out=sb2["ssq4"][:], in_=sb2["ssq4"][:], func=AF.Ln), deps=[q1])
                    q3 = em.op("scalar", lambda A: A.activation(
                        out=sb2["ssq4"][:], in_=sb2["ssq4"][:], func=AF.Exp, scale=-0.5),
                        deps=[q2])
                    b["ssq4_rd"] = q3
                    xg2 = em.op("vector", lambda V: V.tensor_tensor(
                        out=sb2["xgs"][:], in0=sb2["xgs"][:],
                        in1=sb2["normbc"][:], op=ALU.mult),
                        deps=[xg1, tk_nw] + sqs)
                    tps = []
                    for r in range(4):
                        vc = zc * 4 + r
                        xg3 = em.op("vector", lambda V, r=r: V.tensor_scalar(
                            out=sb2["xgs"][:, r * 128:(r + 1) * 128],
                            in0=sb2["xgs"][:, r * 128:(r + 1) * 128],
                            scalar1=sb2["ssq4"][:, r:r + 1], scalar2=None, op0=ALU.mult),
                            deps=[xg2, q3])
                        ptp = pT1 if r % 2 == 0 else pT2
                        xgb = em.op("scalar", lambda A, r=r: A.copy(
                            out=sb2["ln_bf"][:, r * 128:(r + 1) * 128],
                            in_=sb2["xgs"][:, r * 128:(r + 1) * 128]),
                            deps=[xg3, b.get(("lnbf_rd", r))])
                        tp1 = em.op("tensor", lambda T, r=r, ptp=ptp: T.transpose(
                            ptp[:, 0:128], sb2["ln_bf"][:, r * 128:(r + 1) * 128],
                            sb["identb"][:]),
                            deps=[xgb, tk_id, b.get(("tpf2", r % 2)),
                                  b["tp_free"][r % 2]])
                        b[("lnbf_rd", r)] = tp1
                        tp2 = em.op("vector", lambda V, vc=vc, st=st, ptp=ptp: V.tensor_copy(
                            out=sb2["xgT"][:, vc, st * 128:(st + 1) * 128],
                            in_=ptp[:, 0:128]), deps=[tp1, b.get(("xgT_rd", vc))])
                        b[("tpf2", r % 2)] = tp2
                        tps.append(tp2)
                    b["xgs_rd"] = tps[-1]
                    xg_done += tps
                    if st < 7:
                        oco_tok = em.dma("sync", sb2["ocot"][:],
                                         dr["ocore"][s0h + (st + 1) * 128:
                                                     s0h + (st + 2) * 128, :],
                                         deps=[xg1])
                    elif zc < 7:
                        oco_tok = em.dma("sync", sb2["ocot"][:],
                                         dr["ocore"][s0h:s0h + 128, :], deps=[xg1])
                    elif hf == 0:
                        oco_tok = em.dma("sync", sb2["ocot"][:],
                                         dr["ocore"][1024:1152, :], deps=[xg1])
                b["ocot_rd"] = xg1
            b["oco_tok"] = oco_tok
            wo_tok = [None, None]
            wo_free = [wz_free[0], wz_free[1]]
            r3w = []
            for oc in range(4):
                oi = 0
                wo_tok[oi] = em.dma("gpsimd", wzb[0][:],
                                    dr["wout"].rearrange("(c p) o -> p c o", p=128)
                                    [:, :, oc * 512:(oc + 1) * 512],
                                    deps=[wo_free[0], wz_free[1] if oc == 0 else None])
                for st in range(8):
                    sg = s0h + st * 128
                    po = pN["C"] if oc % 2 == 0 else pN["D"]
                    omm = None
                    for vc in range(32):
                        dep0 = [wo_tok[oi], xg_done[-1], b.get(("psO_rd", oc % 2))]                             if vc == 0 else []
                        omm = em.op("tensor", lambda T, vc=vc, st=st, oi=oi, po=po: T.matmul(
                            po[:], sb2["xgT"][:, vc, st * 128:(st + 1) * 128],
                            wzb[oi][:, vc, :], start=(vc == 0), stop=(vc == 31)),
                            deps=dep0, inc=(vc == 31))
                    wo_free[oi] = omm
                    txr = em.dma("sync", sb2["xr2s"][:],
                                 dr["res2"][sg:sg + 128, oc * 512:(oc + 1) * 512],
                                 deps=[b.get("xr2s_rd"), xg_done[-1]])
                    ye = em.op("vector", lambda V, po=po: V.tensor_tensor(
                        out=sb2["oslc"][:], in0=po[:], in1=sb2["xr2s"][:], op=ALU.add),
                        deps=[omm, txr, b.get("oslc_rd")])
                    b[("psO_rd", oc % 2)] = ye
                    b["xr2s_rd"] = ye
                    tdo = em.dma("sync", dr["res3"][sg:sg + 128, oc * 512:(oc + 1) * 512],
                                 sb2["oslc"][:], deps=[ye])
                    b["oslc_rd"] = tdo
                    r3w.append(tdo)
            wz_free[0] = wo_free[0]
            b["tp_free"] = [b.get(("tpf2", 0)), b.get(("tpf2", 1))]
            stage_dma = b.get("stage_dma")
            for st in range(8):
                sg = s0h + st * 128
                tl = em.dma("sync", sb2["r3t"][:], dr["res3"][sg:sg + 128, :],
                            deps=r3w + [b.get("r3t_rd")])
                wt, v6 = emit_ln_tile(em, b, 0, tl, sb2["r3t"][:], sb2["lnstage"],
                                      lnT_war=stage_dma)
                b["r3t_rd"] = v6
                stage_dma = em.dma("sync", dr["xln4T"][:, :, sg:sg + 128],
                                   sb2["lnstage"][:], deps=wt)
            b["stage_dma"] = stage_dma
        em.barrier()
    return nc


def run_phase3(hp, r2):
    nc = bass.Bass("TRN2", target_bir_lowering=False, debug=False)
    dr = {}
    dr["res2"] = nc.dram_tensor("in_res2", [S, E], F32, kind="ExternalInput").ap()
    dr["xln3T"] = nc.dram_tensor("in_xln3T", [128, EC, S], BF16, kind="ExternalInput").ap()
    for nm, dt in (("wqkv", F32), ("wz", F32), ("wba", F32), ("convw", F32),
                   ("alog", F32), ("dtb", F32), ("normw", F32), ("wout", F32),
                   ("ident", BF16), ("identf", F32), ("onesf", F32),
                   ("bdlow", I32), ("bdlowi", I32), ("bdupi", I32),
                   ("m64low", I32), ("m64up", I32), ("identf2", F32)):
        dr[nm] = nc.dram_tensor(
            "in_" + nm, list(hp[nm].shape), dt, kind="ExternalInput").ap()
    dr["res3"] = nc.dram_tensor("res3", [S, E], F32, kind="ExternalOutput").ap()
    dr["xln4T"] = nc.dram_tensor("xln4T", [128, EC, S], BF16, kind="ExternalOutput").ap()
    dr["ocore"] = nc.dram_tensor("ocore", [S, 4096], BF16, kind="ExternalOutput").ap()
    build_phase3(nc, dr)
    in_maps = []
    for c in range(NCORES):
        m = {"in_res2": r2[c]["res2"], "in_xln3T": r2[c]["xln3T"]}
        for nm in ("wqkv", "wz", "wba", "convw", "alog", "dtb", "normw", "wout",
                   "ident", "identf", "onesf", "bdlow", "bdlowi", "bdupi", "m64low", "m64up", "identf2"):
            m["in_" + nm] = hp[nm]
        in_maps.append(m)
    res = run_bass_kernel_spmd(nc, in_maps, core_ids=list(range(NCORES)))
    return res.results
